# version 12
# baseline (speedup 1.0000x reference)
import numpy as np
import ml_dtypes
from contextlib import ExitStack

import concourse.bass as bass
import concourse.mybir as mybir
from concourse.bass_utils import run_bass_kernel_spmd

F32 = mybir.dt.float32
BF16 = mybir.dt.bfloat16
U8 = mybir.dt.uint8
AF = mybir.ActivationFunctionType
ALU = mybir.AluOpType
AX = mybir.AxisListType

NH, HS, H, IFF, S, B, TOWN = 16, 128, 2048, 8192, 4096, 4, 2048
NCORE = 8
EPS = 1e-5
NEG = -1e30
GROUPS_A = [[0, 1, 2, 3], [12, 13, 14, 15], [16, 17, 18, 19], [28, 29, 30, 31]]
GROUPS_B = [[4, 5, 6, 7], [8, 9, 10, 11], [20, 21, 22, 23], [24, 25, 26, 27]]
NPH = [1, 2, 3, 4]
SLOT_KEYS = [1024, 2048, 3072, 4096]


def _ln_sizes():
    pass


def build_nc():
    nc = bass.Bass("TRN2", target_bir_lowering=False)

    x_seq = nc.dram_tensor("x_seq", [S, H], F32, kind="ExternalInput")
    x_own = nc.dram_tensor("x_own", [TOWN, H], F32, kind="ExternalInput")
    qkv_w = nc.dram_tensor("qkv_w", [H, 3 * H], F32, kind="ExternalInput")
    qkv_bp = nc.dram_tensor("qkv_bp", [128, 48], F32, kind="ExternalInput")
    out_w = nc.dram_tensor("out_w", [H, H], F32, kind="ExternalInput")
    out_b2 = nc.dram_tensor("out_b2", [2, H], F32, kind="ExternalInput")
    gate_u8 = nc.dram_tensor("gate_u8", [H, IFF], U8, kind="ExternalInput")
    up_u8 = nc.dram_tensor("up_u8", [H, IFF], U8, kind="ExternalInput")
    down_u8 = nc.dram_tensor("down_u8", [IFF, H], U8, kind="ExternalInput")
    sg_p = nc.dram_tensor("sg_p", [128, 64], F32, kind="ExternalInput")
    su_p = nc.dram_tensor("su_p", [128, 64], F32, kind="ExternalInput")
    sd_p = nc.dram_tensor("sd_p", [128, 16], F32, kind="ExternalInput")
    masks_in = nc.dram_tensor("masks_in", [128, 16 * 1024], BF16, kind="ExternalInput")
    msku_in = nc.dram_tensor("msku_in", [128, 128], BF16, kind="ExternalInput")
    alibi_in = nc.dram_tensor("alibi_in", [NH, 2, S], BF16, kind="ExternalInput")
    ident_in = nc.dram_tensor("ident_in", [128, 128], BF16, kind="ExternalInput")
    ones2_in = nc.dram_tensor("ones2_in", [2, 128], BF16, kind="ExternalInput")
    onesc_in = nc.dram_tensor("onesc_in", [128, 1], BF16, kind="ExternalInput")
    out_d = nc.dram_tensor("out", [TOWN, H], F32, kind="ExternalOutput")

    ln1x_s = nc.dram_tensor("ln1x_s", [S, H], BF16)
    ln1xo_s = nc.dram_tensor("ln1xo_s", [TOWN, H], BF16)
    k_s = nc.dram_tensor("k_s", [NH, 128, S], BF16)
    v_s = nc.dram_tensor("v_s", [NH, 128, S], BF16)
    q_s = nc.dram_tensor("q_s", [NH, 128, TOWN], BF16)
    ctx_s = nc.dram_tensor("ctx_s", [NH, 128, TOWN], BF16)
    x2_s = nc.dram_tensor("x2_s", [TOWN, H], F32)
    ln3x_s = nc.dram_tensor("ln3x_s", [TOWN, H], BF16)

    with ExitStack() as ctx:
        sb = lambda n, sh, dt: ctx.enter_context(nc.sbuf_tensor(n, sh, dt))
        xt = sb("a_xt", [128, 2 * H], F32)
        jnk = sb("a_jnk", [128, H], BF16)
        lnx = sb("a_lnx", [128, 4 * H], BF16)
        st8 = sb("a_st", [128, 8], F32)
        ald2 = [ctx.enter_context(nc.semaphore(f"a_ld{i}")) for i in range(2)]
        avs = ctx.enter_context(nc.semaphore("a_vs"))
        asc = ctx.enter_context(nc.semaphore("a_sc"))
        ast4 = [ctx.enter_context(nc.semaphore(f"a_st{i}")) for i in range(4)]
        blk = ctx.enter_context(nc.Block())
        tiles = [("s", t) for t in range(32)] + [("o", t) for t in range(16)]

        @blk.scalar
        def _(s):
            for t in range(len(tiles)):
                xs = xt[:, (t % 2) * H:((t % 2) + 1) * H]
                s.wait_ge(ald2[t % 2], 16 * (t // 2 + 1))
                s.drain()
                s.activation(jnk[:, :], xs, AF.Square,
                             accum_out=st8[:, 1:2]).then_inc(asc, 1)
                s.wait_ge(avs, 3 * t + 2)
                s.activation(st8[:, 5:6], st8[:, 4:5],
                             AF.Sqrt).then_inc(asc, 1)
                s.wait_ge(avs, 3 * t + 3)
                if t >= 4:
                    s.wait_ge(ast4[t % 4], 16 * (t // 4))
                s.activation(lnx[:, (t % 4) * H:((t % 4) + 1) * H], xs, AF.Identity,
                             bias=st8[:, 7:8], scale=st8[:, 6:7]).then_inc(asc, 1)

        @blk.vector
        def _(v):
            for t in range(len(tiles)):
                xs = xt[:, (t % 2) * H:((t % 2) + 1) * H]
                v.wait_ge(ald2[t % 2], 16 * (t // 2 + 1))
                v.drain()
                v.tensor_reduce(st8[:, 0:1], xs, AX.X, ALU.add,
                                negate=True).then_inc(avs, 1)
                v.wait_ge(asc, 3 * t + 1)
                v.drain()
                v.tensor_scalar(st8[:, 2:3], st8[:, 0:1], 1.0 / H, None, ALU.mult)
                v.tensor_scalar(st8[:, 4:5], st8[:, 1:2], 1.0 / H, EPS, ALU.mult,
                                op1=ALU.add)
                v.drain()
                v.tensor_tensor(st8[:, 3:4], st8[:, 2:3], st8[:, 2:3], ALU.mult)
                v.drain()
                v.tensor_tensor(st8[:, 4:5], st8[:, 4:5], st8[:, 3:4],
                                ALU.subtract).then_inc(avs, 1)
                v.wait_ge(asc, 3 * t + 2)
                v.drain()
                v.reciprocal(st8[:, 6:7], st8[:, 5:6])
                v.drain()
                v.tensor_tensor(st8[:, 7:8], st8[:, 2:3], st8[:, 6:7],
                                ALU.mult).then_inc(avs, 1)

        @blk.sync
        def _(sy):
            for t, (kind, idx) in enumerate(tiles):
                if t >= 2:
                    sy.wait_ge(asc, 3 * (t - 2) + 3)
                src = x_seq if kind == "s" else x_own
                sy.dma_start(xt[:, (t % 2) * H:((t % 2) + 1) * H],
                             src[idx * 128:(idx + 1) * 128, :]).then_inc(ald2[t % 2], 16)
                if t >= 2:
                    tp, (kp, ip) = t - 2, tiles[t - 2]
                    sy.wait_ge(asc, 3 * tp + 3)
                    dst = ln1x_s if kp == "s" else ln1xo_s
                    sy.dma_start(dst[ip * 128:(ip + 1) * 128, :],
                                 lnx[:, (tp % 4) * H:((tp % 4) + 1) * H]
                                 ).then_inc(ast4[tp % 4], 16)
            for t in (len(tiles) - 2, len(tiles) - 1):
                kp, ip = tiles[t]
                sy.wait_ge(asc, 3 * t + 3)
                dst = ln1x_s if kp == "s" else ln1xo_s
                sy.dma_start(dst[ip * 128:(ip + 1) * 128, :],
                             lnx[:, (t % 4) * H:((t % 4) + 1) * H]).then_inc(ast4[t % 4], 16)
            for i4 in range(4):
                sy.wait_ge(ast4[i4], 16 * (len(tiles) // 4))

    def qkv_phase(name, col_base_tile, rhs_scr, n_grp, dst_kind):
        dov = dst_kind == "v"
        with ExitStack() as ctx:
            sb = lambda n, sh, dt: ctx.enter_context(nc.sbuf_tensor(n, sh, dt))
            Wsb = sb(f"{name}_W", [128, 16 * 2048], BF16)
            Wf32 = sb(f"{name}_Wf", [128, 2 * 2048], F32)
            Xg = sb(f"{name}_X", [128, 16 * 2048], BF16)
            stg = sb(f"{name}_st", [128, 2 * 2048], BF16)
            bia = sb(f"{name}_b", [128, 48], F32)
            wld = ctx.enter_context(nc.semaphore(f"{name}_wld"))
            xld = ctx.enter_context(nc.semaphore(f"{name}_xld"))
            mm = ctx.enter_context(nc.semaphore(f"{name}_mm"))
            ev = ctx.enter_context(nc.semaphore(f"{name}_ev"))
            st2 = [ctx.enter_context(nc.semaphore(f"{name}_st{i}")) for i in range(2)]
            if dov:
                stg2 = sb(f"{name}_s2", [128, 2 * 2048], BF16)
                idn = sb(f"{name}_id", [128, 128], BF16)
                tt = ctx.enter_context(nc.semaphore(f"{name}_tt"))
                vv = ctx.enter_context(nc.semaphore(f"{name}_vv"))
                psMa = ctx.enter_context(nc.psum_tensor(f"{name}_pa", [128, 2048], F32))
                pspt = ctx.enter_context(nc.psum_tensor(f"{name}_pt", [128, 2048], BF16))
                psM = [psMa, psMa]
            else:
                psMa = ctx.enter_context(nc.psum_tensor(f"{name}_pa", [128, 2048], F32))
                psMb = ctx.enter_context(nc.psum_tensor(f"{name}_pb", [128, 2048], F32))
                psM = [psMa, psMb]
            blk = ctx.enter_context(nc.Block())
            NK = n_grp * 16

            wcv = ctx.enter_context(nc.semaphore(f"{name}_wcv"))
            wl2 = [ctx.enter_context(nc.semaphore(f"{name}_wl{i}")) for i in range(2)]

            @blk.gpsimd
            def _(g):
                c0 = col_base_tile * 128
                for hh in range(16):
                    if hh >= 2:
                        g.wait_ge(wcv, hh - 1)
                    g.dma_start(Wf32[:, (hh % 2) * 2048:(hh % 2 + 1) * 2048],
                                qkv_w[hh * 128:(hh + 1) * 128, c0:c0 + 2048]
                                ).then_inc(wl2[hh % 2], 16)
                g.dma_start(bia[:, :], qkv_bp[:, :]).then_inc(wld, 16)
                if dov:
                    g.dma_start(idn[:, :], ident_in[:, :]).then_inc(wld, 16)

            @blk.sync
            def _(sy):
                nst = 0
                for grp in range(n_grp):
                    if grp >= 1:
                        sy.wait_ge(mm, 16 * grp)
                    for hh in range(16):
                        for tc in range(4):
                            r0 = grp * 2048 + tc * 512
                            sy.dma_start(
                                Xg[:, hh * 2048 + tc * 512: hh * 2048 + (tc + 1) * 512],
                                rhs_scr[r0:r0 + 512, hh * 128:(hh + 1) * 128],
                                transpose=True).then_inc(xld, 16)
                    for j in range(16):
                        k = grp * 16 + j
                        if dov:
                            sy.wait_ge(vv, 2 * (k + 1))
                            sy.dma_start(v_s[j, :, grp * 2048:(grp + 1) * 2048],
                                         stg2[:, (k % 2) * 2048:(k % 2 + 1) * 2048]
                                         ).then_inc(st2[k % 2], 16)
                        else:
                            sy.wait_ge(ev, k + 1)
                            src = stg[:, (k % 2) * 2048:(k % 2 + 1) * 2048]
                            if dst_kind == "k":
                                sy.dma_start(k_s[j, :, grp * 2048:(grp + 1) * 2048],
                                             src).then_inc(st2[k % 2], 16)
                            else:
                                sy.dma_start(q_s[j, :, :], src).then_inc(st2[k % 2], 16)
                        nst += 1
                for i2 in range(2):
                    sy.wait_ge(st2[i2], 16 * (nst // 2))

            @blk.tensor
            def _(t):
                t.wait_ge(wld, 16 * (2 if dov else 1))
                t.wait_ge(wcv, 16)
                for grp in range(n_grp):
                    t.wait_ge(xld, 16 * 64 * (grp + 1))
                    for j in range(16):
                        k = grp * 16 + j
                        ps = psM[k % 2]
                        if dov:
                            if k >= 1:
                                t.wait_ge(ev, k)
                        elif k >= 2:
                            t.wait_ge(ev, k - 1)
                        for hh in range(16):
                            for tc in range(4):
                                ins = t.matmul(
                                    ps[:, tc * 512:(tc + 1) * 512],
                                    Wsb[:, hh * 2048 + j * 128: hh * 2048 + (j + 1) * 128],
                                    Xg[:, hh * 2048 + tc * 512: hh * 2048 + (tc + 1) * 512],
                                    start=(hh == 0), stop=(hh == 15))
                        ins.then_inc(mm, 1)
                        if dov:
                            t.wait_ge(ev, k + 1)
                            for u in range(16):
                                if u in (0, 8) and k * 2 + u // 8 >= 2:
                                    t.wait_ge(vv, k * 2 + u // 8 - 1)
                                ins2 = t.transpose(
                                    pspt[:, (u // 8) * 1024 + (u % 8) * 128:
                                         (u // 8) * 1024 + (u % 8 + 1) * 128],
                                    stg[:, (k % 2) * 2048 + u * 128:
                                        (k % 2) * 2048 + (u + 1) * 128],
                                    idn[:, :])
                                if u in (7, 15):
                                    ins2.then_inc(tt, 1)

            @blk.scalar
            def _(s):
                for grp in range(n_grp):
                    for j in range(16):
                        k = grp * 16 + j
                        s.wait_ge(mm, k + 1)
                        if k >= 2:
                            s.wait_ge(st2[k % 2], 16 * (k // 2))
                        ps = psM[k % 2]
                        for tc in range(4):
                            ins = s.activation(
                                stg[:, (k % 2) * 2048 + tc * 512:
                                    (k % 2) * 2048 + (tc + 1) * 512],
                                ps[:, tc * 512:(tc + 1) * 512], AF.Identity,
                                bias=bia[:, col_base_tile + j: col_base_tile + j + 1])
                        ins.then_inc(ev, 1)

            @blk.vector
            def _(v):
                for hh in range(16):
                    v.wait_ge(wl2[hh % 2], 16 * (hh // 2 + 1))
                    v.drain()
                    v.tensor_copy(Wsb[:, hh * 2048:(hh + 1) * 2048],
                                  Wf32[:, (hh % 2) * 2048:(hh % 2 + 1) * 2048]
                                  ).then_inc(wcv, 1)
                if not dov:
                    return
                for k in range(NK):
                    for half in range(2):
                        v.wait_ge(tt, 2 * k + half + 1)
                        if k >= 2:
                            v.wait_ge(st2[k % 2], 16 * (k // 2))
                        v.tensor_copy(
                            stg2[:, (k % 2) * 2048 + half * 1024:
                                 (k % 2) * 2048 + (half + 1) * 1024],
                            pspt[:, half * 1024:(half + 1) * 1024]
                        ).then_inc(vv, 1)

    qkv_phase("A2K", 16, ln1x_s, 2, "k")
    qkv_phase("A2V", 32, ln1x_s, 2, "v")
    qkv_phase("A2Q", 0, ln1xo_s, 1, "q")

    with ExitStack() as ctx:
        sb = lambda n, sh, dt: ctx.enter_context(nc.sbuf_tensor(n, sh, dt))
        K2 = sb("b_K", [128, 2 * S], BF16)
        V2 = sb("b_V", [128, 2 * S], BF16)
        Q2 = sb("b_Q", [128, 2 * TOWN], BF16)
        AL = sb("b_al", [2, 2 * S], BF16)
        ON2 = sb("b_on", [2, 128], BF16)
        MSK = sb("b_msk", [128, 16 * 1024], BF16)
        UM = sb("b_um", [128, 128], BF16)
        PB = sb("b_pb", [128, 8 * S], BF16)
        PT = sb("b_ptb", [128, 32 * 512], BF16)
        CTXs = sb("b_ctx", [128, 4 * 512], BF16)
        STT = sb("b_stt", [128, 4 * 32], F32)
        IDN = sb("b_id", [128, 128], BF16)
        hld2 = [ctx.enter_context(nc.semaphore(f"b_hld{i}")) for i in range(2)]
        mld = ctx.enter_context(nc.semaphore("b_mld"))
        mm = ctx.enter_context(nc.semaphore("b_mm"))
        tmm = ctx.enter_context(nc.semaphore("b_tmm"))
        pvm = ctx.enter_context(nc.semaphore("b_pvm"))
        vmx = ctx.enter_context(nc.semaphore("b_vmx"))
        vfin = ctx.enter_context(nc.semaphore("b_vfin"))
        vdif = ctx.enter_context(nc.semaphore("b_vdif"))
        ptv = ctx.enter_context(nc.semaphore("b_ptv"))
        ce = ctx.enter_context(nc.semaphore("b_ce"))
        sex = ctx.enter_context(nc.semaphore("b_sex"))
        sco = ctx.enter_context(nc.semaphore("b_sco"))
        hd = ctx.enter_context(nc.semaphore("b_hd"))
        vsc = ctx.enter_context(nc.semaphore("b_vsc"))
        cst4 = [ctx.enter_context(nc.semaphore(f"b_cst{i}")) for i in range(4)]
        psA = ctx.enter_context(nc.psum_tensor("b_psA", [128, 1024], F32))
        psB = ctx.enter_context(nc.psum_tensor("b_psB", [128, 1024], F32))
        psC = ctx.enter_context(nc.psum_tensor("b_psC", [128, 1024], F32))
        pspt = ctx.enter_context(nc.psum_tensor("b_pspt", [128, 2048], BF16))
        blk = ctx.enter_context(nc.Block())
        psS = [psA, psB, psC]
        psctx = pspt.bitcast(F32)[:, 512:1024]

        @blk.sync
        def _(sy):
            sy.dma_start(MSK[:, :], masks_in[:, :]).then_inc(mld, 16)
            sy.dma_start(IDN[:, :], ident_in[:, :]).then_inc(mld, 16)
            sy.dma_start(ON2[:, :], ones2_in[:, :]).then_inc(mld, 16)
            sy.dma_start(UM[:, :], msku_in[:, :]).then_inc(mld, 16)
            for h in range(NH):
                par = h % 2
                if h >= 2:
                    sy.wait_ge(hd, h - 1)
                sy.dma_start(K2[:, par * S:(par + 1) * S], k_s[h, :, :]).then_inc(hld2[par], 16)
                sy.dma_start(V2[:, par * S:(par + 1) * S], v_s[h, :, :]).then_inc(hld2[par], 16)
                sy.dma_start(Q2[:, par * TOWN:(par + 1) * TOWN],
                             q_s[h, :, :]).then_inc(hld2[par], 16)
                sy.dma_start(AL[:, par * S:(par + 1) * S],
                             alibi_in[h, :, :]).then_inc(hld2[par], 16)
                if h >= 1:
                    for s_ in range(4):
                        gsp = (h - 1) * 4 + s_
                        sy.wait_ge(ce, gsp + 1)
                        sy.dma_start(ctx_s[h - 1, :, s_ * 512:(s_ + 1) * 512],
                                     CTXs[:, (gsp % 4) * 512:(gsp % 4 + 1) * 512]
                                     ).then_inc(cst4[gsp % 4], 16)
            for s_ in range(4):
                gsp = 15 * 4 + s_
                sy.wait_ge(ce, gsp + 1)
                sy.dma_start(ctx_s[15, :, s_ * 512:(s_ + 1) * 512],
                             CTXs[:, (gsp % 4) * 512:(gsp % 4 + 1) * 512]
                             ).then_inc(cst4[gsp % 4], 16)
            for i4 in range(4):
                sy.wait_ge(cst4[i4], 16 * 16)

        @blk.tensor
        def _(t):
            t.wait_ge(mld, 64)
            pc = tg = 0
            slots = [(h, s) for h in range(NH) for s in range(4)]

            def emit_phases(gs):
                nonlocal pc
                h, s_ = slots[gs]
                par = h % 2
                nP = NPH[s_]
                pb0 = (gs % 2) * 4 * S
                if s_ == 0:
                    t.wait_ge(hld2[par], 64 * (h // 2 + 1))
                for r_ in range(4):
                    tile = s_ * 4 + r_
                    qT = Q2[:, par * TOWN + tile * 128: par * TOWN + (tile + 1) * 128]
                    for p_ in range(nP):
                        k0 = p_ * 1024
                        ps = psS[pc % 3]
                        if pc >= 3:
                            t.wait_ge(sex, pc - 2)
                        for c in range(2):
                            t.matmul(ps[:, c * 512:(c + 1) * 512], ON2[:, :],
                                     AL[:, par * S + k0 + c * 512:
                                        par * S + k0 + (c + 1) * 512],
                                     start=True, stop=False)
                        if p_ == nP - 1:
                            for c in range(2):
                                t.matmul(ps[:, c * 512:(c + 1) * 512], UM[:, :],
                                         MSK[:, tile * 1024 + c * 512:
                                             tile * 1024 + (c + 1) * 512],
                                         start=False, stop=False)
                        for c in range(2):
                            ins = t.matmul(
                                ps[:, c * 512:(c + 1) * 512], qT,
                                K2[:, par * S + k0 + c * 512:
                                   par * S + k0 + (c + 1) * 512],
                                start=False, stop=True)
                        ins.then_inc(mm, 1)
                        pc += 1

            def emit_tail(gs):
                nonlocal tg
                h, s_ = slots[gs]
                par = h % 2
                nP = NPH[s_]
                pb0 = (gs % 2) * 4 * S
                t.wait_ge(vfin, gs * 4 + 4)
                for tk in range(8 * nP):
                    if tg >= 2:
                        t.wait_ge(ptv, tg - 1)
                    if tk == 1 and gs >= 1:
                        t.wait_ge(ce, gs)
                    for r_ in range(4):
                        ins = t.transpose(
                            pspt[:, (tg % 2) * 1024 + r_ * 128:
                                 (tg % 2) * 1024 + (r_ + 1) * 128],
                            PB[:, pb0 + r_ * S + tk * 128: pb0 + r_ * S + (tk + 1) * 128],
                            IDN[:, :])
                    ins.then_inc(tmm, 1)
                    tg += 1
                t.wait_ge(ptv, tg)
                if gs >= 1:
                    t.wait_ge(ce, gs)
                for tk in range(8 * nP):
                    ins = t.matmul(psctx[:, :],
                                   V2[:, par * S + tk * 128: par * S + (tk + 1) * 128],
                                   PT[:, tk * 512:(tk + 1) * 512],
                                   start=(tk == 0), stop=(tk == 8 * nP - 1))
                ins.then_inc(pvm, 1)
                if s_ == 3:
                    t.drain().then_inc(hd, 1)

            for gs in range(len(slots)):
                emit_phases(gs)
                if gs >= 1:
                    emit_tail(gs - 1)
            emit_tail(len(slots) - 1)

        @blk.vector
        def _(v):
            v.wait_ge(mld, 64)
            pc = tg = 0
            scoc = 0
            slots = [(h, s) for h in range(NH) for s in range(4)]

            def v_maxes(gs):
                nonlocal pc
                h, s_ = slots[gs]
                nP = NPH[s_]
                sex_at = []
                for r_ in range(4):
                    tile = s_ * 4 + r_
                    rb = r_ * 32
                    for p_ in range(nP):
                        v.wait_ge(mm, pc + 1)
                        ps = psS[pc % 3]
                        v.tensor_reduce(STT[:, rb + p_: rb + p_ + 1], ps[:, :],
                                        AX.X, ALU.max, negate=True).then_inc(vmx, 1)
                        pc += 1
                    sex_at.append(pc)
                return sex_at

            def v_finals(gs, sex_at):
                nonlocal scoc
                h, s_ = slots[gs]
                nP = NPH[s_]
                pb0 = (gs % 2) * 4 * S
                for r_ in range(4):
                    rb = r_ * 32
                    v.wait_ge(sex, sex_at[r_])
                    if nP == 1:
                        v.drain()
                        v.reciprocal(STT[:, rb + 19: rb + 20], STT[:, rb + 4: rb + 5])
                        v.drain()
                        v.tensor_scalar(PB[:, pb0 + r_ * S: pb0 + r_ * S + 1024],
                                        PB[:, pb0 + r_ * S: pb0 + r_ * S + 1024],
                                        STT[:, rb + 19: rb + 20], None,
                                        ALU.mult).then_inc(vfin, 1)
                    else:
                        v.drain()
                        v.tensor_reduce(STT[:, rb + 8: rb + 9], STT[:, rb: rb + nP],
                                        AX.X, ALU.min)
                        v.drain()
                        v.tensor_scalar(STT[:, rb + 10: rb + 10 + nP],
                                        STT[:, rb: rb + nP],
                                        STT[:, rb + 8: rb + 9], -1.0,
                                        ALU.subtract,
                                        op1=ALU.mult).then_inc(vdif, 1)
                        scoc += 1
                        v.wait_ge(sco, scoc)
                        v.drain()
                        v.scalar_tensor_tensor(STT[:, rb + 10: rb + 10 + nP],
                                               STT[:, rb + 14: rb + 14 + nP],
                                               0.0, STT[:, rb + 4: rb + 4 + nP],
                                               ALU.add, ALU.mult,
                                               accum_out=STT[:, rb + 18: rb + 19])
                        v.drain()
                        v.reciprocal(STT[:, rb + 19: rb + 20],
                                     STT[:, rb + 18: rb + 19])
                        v.drain()
                        v.tensor_scalar(STT[:, rb + 14: rb + 14 + nP],
                                        STT[:, rb + 14: rb + 14 + nP],
                                        STT[:, rb + 19: rb + 20], None, ALU.mult)
                        v.drain()
                        for p_ in range(nP):
                            ins = v.tensor_scalar(
                                PB[:, pb0 + r_ * S + p_ * 1024:
                                   pb0 + r_ * S + (p_ + 1) * 1024],
                                PB[:, pb0 + r_ * S + p_ * 1024:
                                   pb0 + r_ * S + (p_ + 1) * 1024],
                                STT[:, rb + 14 + p_: rb + 15 + p_], None, ALU.mult)
                        ins.then_inc(vfin, 1)

            def v_evacs(gs):
                nonlocal tg
                h, s_ = slots[gs]
                nP = NPH[s_]
                for tk in range(8 * nP):
                    v.wait_ge(tmm, tg + 1)
                    v.tensor_copy(PT[:, tk * 512:(tk + 1) * 512],
                                  pspt[:, (tg % 2) * 1024:
                                       (tg % 2) * 1024 + 512]).then_inc(ptv, 1)
                    tg += 1

            def v_ctx(gs):
                v.wait_ge(pvm, gs + 1)
                if gs >= 4:
                    v.wait_ge(cst4[gs % 4], 16 * (gs // 4))
                v.tensor_copy(CTXs[:, (gs % 4) * 512:(gs % 4 + 1) * 512],
                              psctx[:, :]).then_inc(ce, 1)

            for gs in range(len(slots)):
                sx_at = v_maxes(gs)
                if gs >= 1:
                    v_evacs(gs - 1)
                v_finals(gs, sx_at)
                if gs >= 1:
                    v_ctx(gs - 1)
            v_evacs(len(slots) - 1)
            v_ctx(len(slots) - 1)

        @blk.scalar
        def _(s):
            pc = 0
            scoc = 0
            for h in range(NH):
                for s_ in range(4):
                    nP = NPH[s_]
                    pb0 = ((h * 4 + s_) % 2) * 4 * S
                    for r_ in range(4):
                        rb = r_ * 32
                        for p_ in range(nP):
                            s.wait_ge(vmx, pc + 1)
                            ps = psS[pc % 3]
                            s.activation(PB[:, pb0 + r_ * S + p_ * 1024:
                                            pb0 + r_ * S + (p_ + 1) * 1024],
                                         ps[:, :], AF.Exp,
                                         bias=STT[:, rb + p_: rb + p_ + 1],
                                         accum_out=STT[:, rb + 4 + p_: rb + 5 + p_]
                                         ).then_inc(sex, 1)
                            pc += 1
                    if nP > 1:
                        for r_ in range(4):
                            rb = r_ * 32
                            scoc += 1
                            s.wait_ge(vdif, scoc)
                            s.activation(STT[:, rb + 14: rb + 14 + nP],
                                         STT[:, rb + 10: rb + 10 + nP],
                                         AF.Exp).then_inc(sco, 1)

    with ExitStack() as ctx:
        sb = lambda n, sh, dt: ctx.enter_context(nc.sbuf_tensor(n, sh, dt))
        CTXf = sb("c_ctx", [128, 16 * 2048], BF16)
        OW = sb("c_ow", [128, 16 * 2048], BF16)
        OWf = sb("c_owf", [128, 2 * 2048], F32)
        XO = sb("c_xo", [128, 2 * 2048], F32)
        ATT = sb("c_att", [128, 2048], F32)
        LN2A = sb("c_l2", [128, 2048], F32)
        X2T = sb("c_x2", [128, 2048], F32)
        LN3X = sb("c_l3", [128, 2 * 2048], BF16)
        OB2 = sb("c_ob", [2, 2048], BF16)
        ON2c = sb("c_on", [2, 128], BF16)
        jnkC = sb("c_jnk", [128, 2048], BF16)
        st8 = sb("c_st", [128, 8], F32)
        cld2 = [ctx.enter_context(nc.semaphore(f"c_ld{i}")) for i in range(2)]
        wldc = ctx.enter_context(nc.semaphore("c_wld"))
        gwl = ctx.enter_context(nc.semaphore("c_gwl"))
        owc = ctx.enter_context(nc.semaphore("c_owc"))
        owl2 = [ctx.enter_context(nc.semaphore(f"c_owl{i}")) for i in range(2)]
        mmc = ctx.enter_context(nc.semaphore("c_mm"))
        cs = ctx.enter_context(nc.semaphore("c_cs"))
        cv = ctx.enter_context(nc.semaphore("c_cv"))
        xst = ctx.enter_context(nc.semaphore("c_xst"))
        lst2 = [ctx.enter_context(nc.semaphore(f"c_lst{i}")) for i in range(2)]
        ps4a = ctx.enter_context(nc.psum_tensor("c_pa", [128, 2048], F32))
        ps4b = ctx.enter_context(nc.psum_tensor("c_pb", [128, 2048], F32))
        blk = ctx.enter_context(nc.Block())
        ps4 = [ps4a, ps4b]

        @blk.gpsimd
        def _(g):
            for cf in range(16):
                if cf >= 2:
                    g.wait_ge(owc, cf - 1)
                g.dma_start(OWf[:, (cf % 2) * 2048:(cf % 2 + 1) * 2048],
                            out_w[cf * 128:(cf + 1) * 128, :]).then_inc(owl2[cf % 2], 16)
            g.dma_start(OB2[:, :], out_b2[:, :]).then_inc(gwl, 16)

        @blk.sync
        def _(sy):
            for hh in range(16):
                sy.dma_start(CTXf[:, hh * 2048:(hh + 1) * 2048],
                             ctx_s[hh, :, :]).then_inc(wldc, 16)
            sy.dma_start(ON2c[:, :], ones2_in[:, :]).then_inc(wldc, 16)
            for tt in range(17):
                if tt < 16:
                    if tt >= 2:
                        sy.wait_ge(cv, 7 * (tt - 2) + 4)
                    sy.dma_start(XO[:, (tt % 2) * 2048:((tt % 2) + 1) * 2048],
                                 x_own[tt * 128:(tt + 1) * 128, :]).then_inc(cld2[tt % 2], 16)
                if tt >= 1:
                    tp = tt - 1
                    sy.wait_ge(cv, 7 * tp + 4)
                    sy.dma_start(x2_s[tp * 128:(tp + 1) * 128, :],
                                 X2T[:, :]).then_inc(xst, 16)
                    sy.wait_ge(cs, 7 * tp + 7)
                    sy.dma_start(ln3x_s[tp * 128:(tp + 1) * 128, :],
                                 LN3X[:, (tp % 2) * 2048:((tp % 2) + 1) * 2048]
                                 ).then_inc(lst2[tp % 2], 16)
            sy.wait_ge(xst, 16 * 16)
            for i2 in range(2):
                sy.wait_ge(lst2[i2], 16 * 8)

        @blk.tensor
        def _(t):
            t.wait_ge(wldc, 16 * 17)
            t.wait_ge(gwl, 16 * 1)
            t.wait_ge(owc, 16)
            for tt in range(16):
                ps = ps4[tt % 2]
                if tt >= 2:
                    t.wait_ge(cs, 7 * (tt - 2) + 1)
                for ho in range(4):
                    t.matmul(ps[:, ho * 512:(ho + 1) * 512], ON2c[:, :],
                             OB2[:, ho * 512:(ho + 1) * 512], start=True, stop=False)
                for cf in range(16):
                    for ho in range(4):
                        ins = t.matmul(
                            ps[:, ho * 512:(ho + 1) * 512],
                            CTXf[:, cf * 2048 + tt * 128: cf * 2048 + (tt + 1) * 128],
                            OW[:, cf * 2048 + ho * 512: cf * 2048 + (ho + 1) * 512],
                            start=False, stop=(cf == 15))
                ins.then_inc(mmc, 1)

        @blk.scalar
        def _(s):
            for tt in range(16):
                ps = ps4[tt % 2]
                cs0, cv0 = 7 * tt, 7 * tt
                s.wait_ge(mmc, tt + 1)
                if tt >= 1:
                    s.wait_ge(cv, 7 * (tt - 1) + 1)
                s.drain()
                for ho in range(4):
                    ins = s.activation(ATT[:, ho * 512:(ho + 1) * 512],
                                       ps[:, ho * 512:(ho + 1) * 512], AF.Copy)
                ins.then_inc(cs, 1)
                s.drain()
                s.activation(jnkC[:, :], ATT[:, :], AF.Square,
                             accum_out=st8[:, 1:2]).then_inc(cs, 1)
                s.wait_ge(cv, cv0 + 2)
                s.activation(st8[:, 5:6], st8[:, 4:5],
                             AF.Sqrt).then_inc(cs, 1)
                s.wait_ge(cv, cv0 + 3)
                s.activation(LN2A[:, :], ATT[:, :], AF.Identity,
                             bias=st8[:, 7:8], scale=st8[:, 6:7]).then_inc(cs, 1)
                s.wait_ge(cv, cv0 + 4)
                s.drain()
                s.activation(jnkC[:, :], X2T[:, :], AF.Square,
                             accum_out=st8[:, 1:2]).then_inc(cs, 1)
                s.wait_ge(cv, cv0 + 6)
                s.activation(st8[:, 5:6], st8[:, 4:5],
                             AF.Sqrt).then_inc(cs, 1)
                s.wait_ge(cv, cv0 + 7)
                if tt >= 2:
                    s.wait_ge(lst2[tt % 2], 16 * (tt // 2))
                s.activation(LN3X[:, (tt % 2) * 2048:((tt % 2) + 1) * 2048],
                             X2T[:, :], AF.Identity,
                             bias=st8[:, 7:8], scale=st8[:, 6:7]).then_inc(cs, 1)

        @blk.vector
        def _(v):
            for cf in range(16):
                v.wait_ge(owl2[cf % 2], 16 * (cf // 2 + 1))
                v.drain()
                v.tensor_copy(OW[:, cf * 2048:(cf + 1) * 2048],
                              OWf[:, (cf % 2) * 2048:(cf % 2 + 1) * 2048]
                              ).then_inc(owc, 1)
            for tt in range(16):
                cs0, cv0 = 7 * tt, 7 * tt
                v.wait_ge(cs, cs0 + 1)
                v.drain()
                v.tensor_reduce(st8[:, 0:1], ATT[:, :], AX.X, ALU.add,
                                negate=True).then_inc(cv, 1)
                v.wait_ge(cs, cs0 + 2)
                v.drain()
                v.tensor_scalar(st8[:, 2:3], st8[:, 0:1], 1.0 / H, None, ALU.mult)
                v.tensor_scalar(st8[:, 4:5], st8[:, 1:2], 1.0 / H, EPS, ALU.mult,
                                op1=ALU.add)
                v.drain()
                v.tensor_tensor(st8[:, 3:4], st8[:, 2:3], st8[:, 2:3], ALU.mult)
                v.drain()
                v.tensor_tensor(st8[:, 4:5], st8[:, 4:5], st8[:, 3:4],
                                ALU.subtract).then_inc(cv, 1)
                v.wait_ge(cs, cs0 + 3)
                v.drain()
                v.reciprocal(st8[:, 6:7], st8[:, 5:6])
                v.drain()
                v.tensor_tensor(st8[:, 7:8], st8[:, 2:3], st8[:, 6:7],
                                ALU.mult).then_inc(cv, 1)
                v.wait_ge(cs, cs0 + 4)
                v.wait_ge(cld2[tt % 2], 16 * (tt // 2 + 1))
                if tt >= 1:
                    v.wait_ge(xst, 16 * tt)
                    v.wait_ge(cs, 7 * (tt - 1) + 7)
                v.drain()
                v.tensor_tensor(X2T[:, :], XO[:, (tt % 2) * 2048:((tt % 2) + 1) * 2048],
                                LN2A[:, :], ALU.add).then_inc(cv, 1)
                v.drain()
                v.tensor_reduce(st8[:, 0:1], X2T[:, :], AX.X, ALU.add,
                                negate=True).then_inc(cv, 1)
                v.wait_ge(cs, cs0 + 5)
                v.drain()
                v.tensor_scalar(st8[:, 2:3], st8[:, 0:1], 1.0 / H, None, ALU.mult)
                v.tensor_scalar(st8[:, 4:5], st8[:, 1:2], 1.0 / H, EPS, ALU.mult,
                                op1=ALU.add)
                v.drain()
                v.tensor_tensor(st8[:, 3:4], st8[:, 2:3], st8[:, 2:3], ALU.mult)
                v.drain()
                v.tensor_tensor(st8[:, 4:5], st8[:, 4:5], st8[:, 3:4],
                                ALU.subtract).then_inc(cv, 1)
                v.wait_ge(cs, cs0 + 6)
                v.drain()
                v.reciprocal(st8[:, 6:7], st8[:, 5:6])
                v.drain()
                v.tensor_tensor(st8[:, 7:8], st8[:, 2:3], st8[:, 6:7],
                                ALU.mult).then_inc(cv, 1)

    with ExitStack() as ctx:
        sb = lambda n, sh, dt: ctx.enter_context(nc.sbuf_tensor(n, sh, dt))
        L3T = sb("d_l3", [128, 16 * 512], BF16)
        WG = sb("d_wg", [128, 2 * 8192], BF16)
        WU = sb("d_wu", [128, 2 * 8192], BF16)
        HA = sb("d_ha", [128, 64 * 512], BF16)
        WD = sb("d_wd", [128, 16 * 512], BF16)
        DFT = sb("d_dft", [128, 16 * 512], BF16)
        DTT = sb("d_dtt", [128, 2048], BF16)
        X2R = sb("d_x2", [128, 2048], F32)
        OUTT = sb("d_out", [128, 2048], F32)
        LN4A = sb("d_l4", [128, 2048], BF16)
        GACT = sb("d_ga", [128, 512], BF16)
        UACT = sb("d_ua", [128, 512], BF16)
        NCSh = sb("d_ncsh", [1, 512], BF16)
        NCSl = sb("d_ncsl", [1, 512], BF16)
        NDh, NDl = NCSh, NCSl
        ON2d = sb("d_on2", [2, 128], BF16)
        ONC = sb("d_onc", [128, 1], BF16)
        SGs = sb("d_sg", [128, 64], F32)
        SUs = sb("d_su", [128, 64], F32)
        SDs = sb("d_sd", [128, 16], F32)
        IDNd = sb("d_id", [128, 128], BF16)
        st8d = sb("d_st", [128, 8], F32)
        l3l = ctx.enter_context(nc.semaphore("d_l3l"))
        wgl2 = [ctx.enter_context(nc.semaphore(f"d_wgl{i}")) for i in range(2)]
        dwl2 = [ctx.enter_context(nc.semaphore(f"d_dwl{i}")) for i in range(2)]
        xl2 = [ctx.enter_context(nc.semaphore(f"d_xl{i}")) for i in range(2)]
        gm = ctx.enter_context(nc.semaphore("d_gm"))
        csm = ctx.enter_context(nc.semaphore("d_csm"))
        dcs = ctx.enter_context(nc.semaphore("d_dcs"))
        sgd = ctx.enter_context(nc.semaphore("d_sgd"))
        hv = ctx.enter_context(nc.semaphore("d_hv"))
        dmi = ctx.enter_context(nc.semaphore("d_dmi"))
        dm = ctx.enter_context(nc.semaphore("d_dm"))
        dv = ctx.enter_context(nc.semaphore("d_dv"))
        tm = ctx.enter_context(nc.semaphore("d_tm"))
        vE = ctx.enter_context(nc.semaphore("d_vE"))
        sv4 = ctx.enter_context(nc.semaphore("d_sv4"))
        ss4 = ctx.enter_context(nc.semaphore("d_ss4"))
        fo = ctx.enter_context(nc.semaphore("d_fo"))
        mldd = ctx.enter_context(nc.semaphore("d_mld"))
        psGU = ctx.enter_context(nc.psum_tensor("d_pgu", [128, 2048], F32))
        psD = psGU
        psE = ctx.enter_context(nc.psum_tensor("d_pe", [128, 2048], BF16))
        blk = ctx.enter_context(nc.Block())

        def ap3(tensor, off, dims):
            return bass.AP(tensor, off, dims)

        @blk.gpsimd
        def _(g):
            g.dma_start(SGs[:, :], sg_p[:, :]).then_inc(mldd, 16)
            g.dma_start(SUs[:, :], su_p[:, :]).then_inc(mldd, 16)
            g.dma_start(SDs[:, :], sd_p[:, :]).then_inc(mldd, 16)
            g.dma_start(IDNd[:, :], ident_in[:, :]).then_inc(mldd, 16)
            g.dma_start(ON2d[:, :], ones2_in[:, :]).then_inc(mldd, 16)
            g.dma_start(ONC[:, :], onesc_in[:, :]).then_inc(mldd, 16)
            sxg = 0
            ig = 0
            for tc in range(4):
                for sx in range(16):
                    if sxg >= 2:
                        g.wait_ge(gm, 8 * (sxg - 1))
                    for (scr, dstt) in ((gate_u8, WG), (up_u8, WU)):
                        in_ap = bass.AP(scr, sx * 512,
                                        [[IFF, 128], [128 * IFF, 16], [1, 512]])
                        out_ap = bass.AP(dstt, (sxg % 2) * 8192,
                                         [[2 * 8192, 128], [512, 16], [1, 512]])
                        g.dma_start(out_ap, in_ap).then_inc(wgl2[sxg % 2], 16)
                    sxg += 1
                for q in range(4):
                    for ib in range(8):
                        if ig >= 2:
                            g.wait_ge(dmi, ig - 1)
                        for ii in range(8):
                            i = ib * 8 + ii
                            g.dma_start(
                                WD[:, ((ig % 2) * 8 + ii) * 512:
                                   ((ig % 2) * 8 + ii + 1) * 512],
                                down_u8[i * 128:(i + 1) * 128, q * 512:(q + 1) * 512]
                            ).then_inc(dwl2[ig % 2], 16)
                        ig += 1

        @blk.sync
        def _(sy):
            tsg = 0
            for tc in range(4):
                if tc >= 1:
                    sy.wait_ge(gm, 128 * tc)
                for hh in range(16):
                    sy.dma_start(L3T[:, hh * 512:(hh + 1) * 512],
                                 ln3x_s[tc * 512:(tc + 1) * 512, hh * 128:(hh + 1) * 128],
                                 transpose=True).then_inc(l3l, 16)
                for ts_ in range(4):
                    if tsg >= 1:
                        sy.wait_ge(sv4, 4 * (tsg - 1) + 4)
                    sy.dma_start(X2R[:, :],
                                 x2_s[tsg * 128:(tsg + 1) * 128, :]).then_inc(xl2[tsg % 2], 16)
                    sy.wait_ge(sv4, 4 * tsg + 4)
                    sy.dma_start(out_d[tsg * 128:(tsg + 1) * 128, :],
                                 OUTT[:, :]).then_inc(fo, 16)
                    tsg += 1
            sy.wait_ge(fo, 16 * 16)

        @blk.tensor
        def _(t):
            t.wait_ge(mldd, 96)
            igl = 0
            qg = 0
            idg = 0
            tsg = 0
            for tc in range(4):
                t.wait_ge(l3l, 256 * (tc + 1))
                if tc >= 1:
                    t.wait_ge(dv, 16 * tc)
                for hh in range(16):
                    ins = t.matmul(psD[0:1, 0:512], ONC[:, :],
                                   L3T[:, hh * 512:(hh + 1) * 512],
                                   start=(hh == 0), stop=(hh == 15))
                ins.then_inc(csm, 1)
                t.wait_ge(dcs, 2 * tc + 1)
                for sx in range(16):
                    sxg = tc * 16 + sx
                    t.wait_ge(wgl2[sxg % 2], 32 * (sxg // 2 + 1))
                    for i in range(4):
                        pp = igl % 2
                        psG = psGU[:, (2 * pp) * 512:(2 * pp + 1) * 512]
                        psU = psGU[:, (2 * pp + 1) * 512:(2 * pp + 2) * 512]
                        if igl >= 2:
                            t.wait_ge(sgd, igl - 1)
                            t.wait_ge(hv, igl - 1)
                        t.matmul(psG, ON2d[0:1, :], NCSh[0:1, :],
                                 start=True, stop=False)
                        t.matmul(psG, ON2d[0:1, :], NCSl[0:1, :],
                                 start=False, stop=False)
                        for hh in range(16):
                            ins = t.matmul(psG,
                                           WG[:, (sxg % 2) * 8192 + hh * 512 + i * 128:
                                              (sxg % 2) * 8192 + hh * 512 + (i + 1) * 128],
                                           L3T[:, hh * 512:(hh + 1) * 512],
                                           start=False, stop=(hh == 15))
                        ins.then_inc(gm, 1)
                        t.matmul(psU, ON2d[0:1, :], NCSh[0:1, :],
                                 start=True, stop=False)
                        t.matmul(psU, ON2d[0:1, :], NCSl[0:1, :],
                                 start=False, stop=False)
                        for hh in range(16):
                            ins = t.matmul(psU,
                                           WU[:, (sxg % 2) * 8192 + hh * 512 + i * 128:
                                              (sxg % 2) * 8192 + hh * 512 + (i + 1) * 128],
                                           L3T[:, hh * 512:(hh + 1) * 512],
                                           start=False, stop=(hh == 15))
                        ins.then_inc(gm, 1)
                        igl += 1
                t.wait_ge(hv, 64 * (tc + 1))
                for ic in range(64):
                    ins = t.matmul(psD[0:1, 0:512], ONC[:, :],
                                   HA[:, ic * 512:(ic + 1) * 512],
                                   start=(ic == 0), stop=(ic == 63))
                ins.then_inc(csm, 1)
                t.wait_ge(dcs, 2 * tc + 2)
                for q in range(4):
                    if qg >= 1:
                        t.wait_ge(dv, 4 * qg)
                    for hm in range(4):
                        t.matmul(psD[:, hm * 512:(hm + 1) * 512], ON2d[0:1, :],
                                 NDh[0:1, :], start=True, stop=False)
                        t.matmul(psD[:, hm * 512:(hm + 1) * 512], ON2d[0:1, :],
                                 NDl[0:1, :], start=False, stop=False)
                    for ib in range(8):
                        t.wait_ge(dwl2[idg % 2], 16 * 8 * (idg // 2 + 1))
                        for hm in range(4):
                            for ii in range(8):
                                i = ib * 8 + ii
                                ins = t.matmul(
                                    psD[:, hm * 512:(hm + 1) * 512],
                                    WD[:, ((idg % 2) * 8 + ii) * 512 + hm * 128:
                                       ((idg % 2) * 8 + ii) * 512 + (hm + 1) * 128],
                                    HA[:, i * 512:(i + 1) * 512],
                                    start=False, stop=(ib == 7 and ii == 7))
                        ins.then_inc(dmi, 1)
                        idg += 1
                    t.drain().then_inc(dm, 1)
                    qg += 1
                t.wait_ge(dv, 16 * (tc + 1))
                for ts_ in range(4):
                    if tsg >= 1:
                        t.wait_ge(vE, 2 * tsg)
                    for u in range(16):
                        ins = t.transpose(
                            psE[:, (u // 8) * 1024 + (u % 8) * 128:
                                (u // 8) * 1024 + (u % 8 + 1) * 128],
                            DFT[:, u * 512 + ts_ * 128: u * 512 + ts_ * 128 + 128],
                            IDNd[:, :])
                        if u in (7, 15):
                            ins.then_inc(tm, 1)
                    tsg += 1

        @blk.scalar
        def _(s):
            s.wait_ge(mldd, 96)
            igl = 0
            qg = 0
            tsg = 0
            for tc in range(4):
                for sx in range(16):
                    for i in range(4):
                        gi = sx * 4 + i
                        pp = igl % 2
                        s.wait_ge(gm, 2 * igl + 1)
                        if igl >= 1:
                            s.wait_ge(hv, igl)
                        s.drain()
                        s.activation(GACT[:, :],
                                     psGU[:, (2 * pp) * 512:(2 * pp + 1) * 512],
                                     AF.Silu,
                                     scale=SGs[:, gi:gi + 1]).then_inc(sgd, 1)
                        igl += 1
                for q in range(4):
                    s.wait_ge(dm, qg + 1)
                    for hm in range(4):
                        hmg = q * 4 + hm
                        ins = s.activation(DFT[:, hmg * 512:(hmg + 1) * 512],
                                           psD[:, hm * 512:(hm + 1) * 512], AF.Copy,
                                           scale=SDs[:, hmg:hmg + 1])
                        ins.then_inc(dv, 1)
                    qg += 1
                for ts_ in range(4):
                    s.wait_ge(vE, 2 * tsg + 2)
                    if tsg >= 1:
                        s.wait_ge(sv4, 4 * (tsg - 1) + 4)
                    s.drain()
                    s.activation(LN4A[:, :],
                                 DTT[:, :],
                                 AF.Square, accum_out=st8d[:, 1:2]).then_inc(ss4, 1)
                    s.wait_ge(sv4, 4 * tsg + 2)
                    s.activation(st8d[:, 5:6], st8d[:, 4:5],
                                 AF.Sqrt).then_inc(ss4, 1)
                    s.wait_ge(sv4, 4 * tsg + 3)
                    s.activation(LN4A[:, :],
                                 DTT[:, :],
                                 AF.Identity, bias=st8d[:, 7:8],
                                 scale=st8d[:, 6:7]).then_inc(ss4, 1)
                    tsg += 1

        @blk.vector
        def _(v):
            v.wait_ge(mldd, 96)
            igl = 0
            tsg = 0
            for tc in range(4):
                v.wait_ge(csm, 2 * tc + 1)
                v.drain()
                v.tensor_scalar(NCSh[0:1, :], psD[0:1, 0:512], -128.0, None, ALU.mult)
                v.drain()
                v.scalar_tensor_tensor(NCSl[0:1, :], psD[0:1, 0:512], -128.0,
                                       NCSh[0:1, :], ALU.mult,
                                       ALU.subtract).then_inc(dcs, 1)
                for sx in range(16):
                    for i in range(4):
                        gi = sx * 4 + i
                        pp = igl % 2
                        v.wait_ge(gm, 2 * igl + 2)
                        v.drain()
                        v.tensor_scalar(UACT[:, :],
                                        psGU[:, (2 * pp + 1) * 512:(2 * pp + 2) * 512],
                                        SUs[:, gi:gi + 1],
                                        None, ALU.mult)
                        v.wait_ge(sgd, igl + 1)
                        v.drain()
                        v.tensor_tensor(HA[:, gi * 512:(gi + 1) * 512], GACT[:, :],
                                        UACT[:, :], ALU.mult).then_inc(hv, 1)
                        igl += 1
                v.wait_ge(csm, 2 * tc + 2)
                v.drain()
                v.tensor_scalar(NDh[0:1, :], psD[0:1, 0:512], -128.0, None, ALU.mult)
                v.drain()
                v.scalar_tensor_tensor(NDl[0:1, :], psD[0:1, 0:512], -128.0,
                                       NDh[0:1, :], ALU.mult,
                                       ALU.subtract).then_inc(dcs, 1)
                for ts_ in range(4):
                    for half in range(2):
                        v.wait_ge(tm, 2 * tsg + half + 1)
                        if tsg >= 1:
                            v.wait_ge(ss4, 3 * tsg)
                        v.tensor_copy(DTT[:, half * 1024:(half + 1) * 1024],
                                      psE[:, half * 1024:(half + 1) * 1024]
                                      ).then_inc(vE, 1)
                    if tsg >= 1:
                        v.wait_ge(ss4, 3 * tsg)
                    v.drain()
                    v.tensor_reduce(st8d[:, 0:1],
                                    DTT[:, :],
                                    AX.X, ALU.add, negate=True).then_inc(sv4, 1)
                    v.wait_ge(ss4, 3 * tsg + 1)
                    v.drain()
                    v.tensor_scalar(st8d[:, 2:3], st8d[:, 0:1], 1.0 / H, None, ALU.mult)
                    v.tensor_scalar(st8d[:, 4:5], st8d[:, 1:2], 1.0 / H, EPS,
                                    ALU.mult, op1=ALU.add)
                    v.drain()
                    v.tensor_tensor(st8d[:, 3:4], st8d[:, 2:3], st8d[:, 2:3], ALU.mult)
                    v.drain()
                    v.tensor_tensor(st8d[:, 4:5], st8d[:, 4:5], st8d[:, 3:4],
                                    ALU.subtract).then_inc(sv4, 1)
                    v.wait_ge(ss4, 3 * tsg + 2)
                    v.drain()
                    v.reciprocal(st8d[:, 6:7], st8d[:, 5:6])
                    v.drain()
                    v.tensor_tensor(st8d[:, 7:8], st8d[:, 2:3], st8d[:, 6:7],
                                    ALU.mult).then_inc(sv4, 1)
                    v.wait_ge(ss4, 3 * tsg + 3)
                    v.wait_ge(xl2[tsg % 2], 16 * (tsg // 2 + 1))
                    if tsg >= 1:
                        v.wait_ge(fo, 16 * tsg)
                    v.drain()
                    v.tensor_tensor(OUTT[:, :],
                                    X2R[:, :],
                                    LN4A[:, :], ALU.add).then_inc(sv4, 1)
                    tsg += 1

    return nc


_NC_CACHE = {}


def _get_nc():
    if "nc" not in _NC_CACHE:
        _NC_CACHE["nc"] = build_nc()
    return _NC_CACHE["nc"]


def _host_prep(inputs):
    x = np.asarray(inputs["L0_attn_x"], dtype=np.float32)
    qkv_w = np.ascontiguousarray(np.asarray(inputs["L0_attn_qkv_w"], np.float32))
    qkv_b = np.asarray(inputs["L0_attn_qkv_b"], np.float32)
    out_w = np.ascontiguousarray(np.asarray(inputs["L0_attn_out_w"], np.float32))
    out_b = np.asarray(inputs["L0_attn_out_b"], np.float32)
    gate_w = np.asarray(inputs["L0_ffn_gate_w"], np.int32)
    up_w = np.asarray(inputs["L0_ffn_up_w"], np.int32)
    down_w = np.asarray(inputs["L0_ffn_down_w"], np.int32)
    gate_sc = np.asarray(inputs["L0_ffn_gate_scale"], np.float32)
    up_sc = np.asarray(inputs["L0_ffn_up_scale"], np.float32)
    down_sc = np.asarray(inputs["L0_ffn_down_scale"], np.float32)

    def u8view(a_i32):
        a = np.ascontiguousarray(a_i32)
        return np.ascontiguousarray(
            a.view(np.uint8).reshape(a.shape[0], a.shape[1], 4)[:, :, 0])

    gate_u8 = u8view(gate_w.T)
    up_u8 = u8view(up_w.T)
    down_u8 = u8view(down_w.T)

    qkv_bp = np.ascontiguousarray(qkv_b.reshape(48, 128).T)
    out_b2 = np.zeros((2, H), np.float32)
    out_b2[0] = out_b
    sg_p = np.ascontiguousarray(gate_sc.reshape(64, 128).T)
    su_p = np.ascontiguousarray(up_sc.reshape(64, 128).T)
    sd_p = np.ascontiguousarray(down_sc.reshape(16, 128).T)

    slopes = np.array([1.0 / 2.0 ** (i / 2.0) for i in range(NH)], np.float32)
    alibi = np.zeros((NH, 2, S), dtype=ml_dtypes.bfloat16)
    j = np.arange(S, dtype=np.float32)
    for h in range(NH):
        a = slopes[h] * j
        hi = a.astype(ml_dtypes.bfloat16)
        lo = (a - hi.astype(np.float32)).astype(ml_dtypes.bfloat16)
        alibi[h, 0] = hi
        alibi[h, 1] = lo
    ident = np.eye(128, dtype=np.float32).astype(ml_dtypes.bfloat16)
    ones2 = np.ones((2, 128), dtype=ml_dtypes.bfloat16)
    onesc = np.ones((128, 1), dtype=ml_dtypes.bfloat16)

    masks = {}
    for par, groups in ((0, GROUPS_A), (1, GROUPS_B)):
        m = np.zeros((128, 16 * 1024), np.float32)
        for s_ in range(4):
            reg0 = SLOT_KEYS[s_] - 1024
            for r_ in range(4):
                tile = s_ * 4 + r_
                g = groups[s_][r_]
                d = g * 128 - reg0
                w = m[:, tile * 1024:(tile + 1) * 1024]
                for jc in range(1024):
                    e = jc - d
                    if 1 <= e <= 127:
                        w[e, jc] = NEG
                    elif e >= 128:
                        w[127, jc] = NEG
                        w[0, jc] = NEG
        masks[par] = np.ascontiguousarray(m.astype(ml_dtypes.bfloat16))
    msk_u = np.zeros((128, 128), np.float32)
    for k in range(128):
        msk_u[k, :k] = 1.0
    msk_u[0, 127] = 1.0
    msk_u = np.ascontiguousarray(msk_u.astype(ml_dtypes.bfloat16))

    in_maps = []
    own_rows_all = []
    for c in range(NCORE):
        b = c // 2
        par = c % 2
        groups = GROUPS_A if par == 0 else GROUPS_B
        own_tiles = [g for grp in groups for g in grp]
        own_rows = np.concatenate([np.arange(g * 128, (g + 1) * 128)
                                   for g in own_tiles])
        own_rows_all.append((b, own_rows))
        xb = np.ascontiguousarray(x[b])
        in_maps.append(dict(
            x_seq=xb,
            x_own=np.ascontiguousarray(xb[own_rows]),
            qkv_w=qkv_w, qkv_bp=qkv_bp, out_w=out_w, out_b2=out_b2,
            gate_u8=gate_u8, up_u8=up_u8, down_u8=down_u8,
            sg_p=sg_p, su_p=su_p, sd_p=sd_p,
            masks_in=masks[par], msku_in=msk_u, alibi_in=alibi,
            ident_in=ident, ones2_in=ones2, onesc_in=onesc,
        ))
    return in_maps, own_rows_all


def kernel(**inputs):
    in_maps, own_rows_all = _host_prep(inputs)
    nc = _get_nc()
    res = run_bass_kernel_spmd(nc, in_maps, core_ids=list(range(NCORE)))
    out = np.zeros((B, S, H), dtype=np.float32)
    for c in range(NCORE):
        b, own_rows = own_rows_all[c]
        out[b, own_rows, :] = np.asarray(res.results[c]["out"], np.float32)
    return out


# version 14
# speedup vs baseline: 1.0571x; 1.0571x over previous
import numpy as np
import ml_dtypes
from contextlib import ExitStack

import concourse.bass as bass
import concourse.mybir as mybir
from concourse.bass_utils import run_bass_kernel_spmd

F32 = mybir.dt.float32
BF16 = mybir.dt.bfloat16
U8 = mybir.dt.uint8
AF = mybir.ActivationFunctionType
ALU = mybir.AluOpType
AX = mybir.AxisListType

NH, HS, H, IFF, S, B, TOWN = 16, 128, 2048, 8192, 4096, 4, 2048
NCORE = 8
EPS = 1e-5
NEG = -1e30
GROUPS_A = [[0, 1, 2, 3], [12, 13, 14, 15], [16, 17, 18, 19], [28, 29, 30, 31]]
GROUPS_B = [[4, 5, 6, 7], [8, 9, 10, 11], [20, 21, 22, 23], [24, 25, 26, 27]]
NPH = [1, 2, 3, 4]
SLOT_KEYS = [1024, 2048, 3072, 4096]


def _ln_sizes():
    pass


def build_nc():
    nc = bass.Bass("TRN2", target_bir_lowering=False)

    x_seq = nc.dram_tensor("x_seq", [S, H], F32, kind="ExternalInput")
    x_own = nc.dram_tensor("x_own", [TOWN, H], F32, kind="ExternalInput")
    qkv_w = nc.dram_tensor("qkv_w", [H, 3 * H], F32, kind="ExternalInput")
    qkv_bp = nc.dram_tensor("qkv_bp", [128, 48], F32, kind="ExternalInput")
    out_w = nc.dram_tensor("out_w", [H, H], F32, kind="ExternalInput")
    out_b2 = nc.dram_tensor("out_b2", [2, H], F32, kind="ExternalInput")
    gate_u8 = nc.dram_tensor("gate_u8", [H, IFF], U8, kind="ExternalInput")
    up_u8 = nc.dram_tensor("up_u8", [H, IFF], U8, kind="ExternalInput")
    down_u8 = nc.dram_tensor("down_u8", [IFF, H], U8, kind="ExternalInput")
    sg_p = nc.dram_tensor("sg_p", [128, 64], F32, kind="ExternalInput")
    su_p = nc.dram_tensor("su_p", [128, 64], F32, kind="ExternalInput")
    sd_p = nc.dram_tensor("sd_p", [128, 16], F32, kind="ExternalInput")
    masks_in = nc.dram_tensor("masks_in", [128, 16 * 1024], BF16, kind="ExternalInput")
    msku_in = nc.dram_tensor("msku_in", [128, 128], BF16, kind="ExternalInput")
    alibi_in = nc.dram_tensor("alibi_in", [NH, 2, S], BF16, kind="ExternalInput")
    ident_in = nc.dram_tensor("ident_in", [128, 128], BF16, kind="ExternalInput")
    ones2_in = nc.dram_tensor("ones2_in", [2, 128], BF16, kind="ExternalInput")
    onesc_in = nc.dram_tensor("onesc_in", [128, 1], BF16, kind="ExternalInput")
    out_d = nc.dram_tensor("out", [TOWN, H], F32, kind="ExternalOutput")

    ln1x_s = nc.dram_tensor("ln1x_s", [S, H], BF16)
    ln1xo_s = nc.dram_tensor("ln1xo_s", [TOWN, H], BF16)
    k_s = nc.dram_tensor("k_s", [NH, 128, S], BF16)
    v_s = nc.dram_tensor("v_s", [NH, 128, S], BF16)
    q_s = nc.dram_tensor("q_s", [NH, 128, TOWN], BF16)
    ctx_s = nc.dram_tensor("ctx_s", [NH, 128, TOWN], BF16)
    x2_s = nc.dram_tensor("x2_s", [TOWN, H], F32)
    ln3x_s = nc.dram_tensor("ln3x_s", [TOWN, H], BF16)

    with ExitStack() as ctx:
        sb = lambda n, sh, dt: ctx.enter_context(nc.sbuf_tensor(n, sh, dt))
        xt = sb("a_xt", [128, 2 * H], F32)
        jnk = sb("a_jnk", [128, H], BF16)
        lnx = sb("a_lnx", [128, 4 * H], BF16)
        st8 = sb("a_st", [128, 8], F32)
        ald2 = [ctx.enter_context(nc.semaphore(f"a_ld{i}")) for i in range(2)]
        avs = ctx.enter_context(nc.semaphore("a_vs"))
        asc = ctx.enter_context(nc.semaphore("a_sc"))
        ast4 = [ctx.enter_context(nc.semaphore(f"a_st{i}")) for i in range(4)]
        blk = ctx.enter_context(nc.Block())
        tiles = [("s", t) for t in range(32)] + [("o", t) for t in range(16)]

        @blk.scalar
        def _(s):
            for t in range(len(tiles)):
                xs = xt[:, (t % 2) * H:((t % 2) + 1) * H]
                s.wait_ge(ald2[t % 2], 16 * (t // 2 + 1))
                s.drain()
                s.activation(jnk[:, :], xs, AF.Square,
                             accum_out=st8[:, 1:2]).then_inc(asc, 1)
                s.wait_ge(avs, 3 * t + 2)
                s.activation(st8[:, 5:6], st8[:, 4:5],
                             AF.Sqrt).then_inc(asc, 1)
                s.wait_ge(avs, 3 * t + 3)
                if t >= 4:
                    s.wait_ge(ast4[t % 4], 16 * (t // 4))
                s.activation(lnx[:, (t % 4) * H:((t % 4) + 1) * H], xs, AF.Identity,
                             bias=st8[:, 7:8], scale=st8[:, 6:7]).then_inc(asc, 1)

        @blk.vector
        def _(v):
            for t in range(len(tiles)):
                xs = xt[:, (t % 2) * H:((t % 2) + 1) * H]
                v.wait_ge(ald2[t % 2], 16 * (t // 2 + 1))
                v.drain()
                v.tensor_reduce(st8[:, 0:1], xs, AX.X, ALU.add,
                                negate=True).then_inc(avs, 1)
                v.wait_ge(asc, 3 * t + 1)
                v.drain()
                v.tensor_scalar(st8[:, 2:3], st8[:, 0:1], 1.0 / H, None, ALU.mult)
                v.tensor_scalar(st8[:, 4:5], st8[:, 1:2], 1.0 / H, EPS, ALU.mult,
                                op1=ALU.add)
                v.drain()
                v.tensor_tensor(st8[:, 3:4], st8[:, 2:3], st8[:, 2:3], ALU.mult)
                v.drain()
                v.tensor_tensor(st8[:, 4:5], st8[:, 4:5], st8[:, 3:4],
                                ALU.subtract).then_inc(avs, 1)
                v.wait_ge(asc, 3 * t + 2)
                v.drain()
                v.reciprocal(st8[:, 6:7], st8[:, 5:6])
                v.drain()
                v.tensor_tensor(st8[:, 7:8], st8[:, 2:3], st8[:, 6:7],
                                ALU.mult).then_inc(avs, 1)

        @blk.sync
        def _(sy):
            for t, (kind, idx) in enumerate(tiles):
                if t >= 2:
                    sy.wait_ge(asc, 3 * (t - 2) + 3)
                src = x_seq if kind == "s" else x_own
                sy.dma_start(xt[:, (t % 2) * H:((t % 2) + 1) * H],
                             src[idx * 128:(idx + 1) * 128, :]).then_inc(ald2[t % 2], 16)
                if t >= 2:
                    tp, (kp, ip) = t - 2, tiles[t - 2]
                    sy.wait_ge(asc, 3 * tp + 3)
                    dst = ln1x_s if kp == "s" else ln1xo_s
                    sy.dma_start(dst[ip * 128:(ip + 1) * 128, :],
                                 lnx[:, (tp % 4) * H:((tp % 4) + 1) * H]
                                 ).then_inc(ast4[tp % 4], 16)
            for t in (len(tiles) - 2, len(tiles) - 1):
                kp, ip = tiles[t]
                sy.wait_ge(asc, 3 * t + 3)
                dst = ln1x_s if kp == "s" else ln1xo_s
                sy.dma_start(dst[ip * 128:(ip + 1) * 128, :],
                             lnx[:, (t % 4) * H:((t % 4) + 1) * H]).then_inc(ast4[t % 4], 16)
            for i4 in range(4):
                sy.wait_ge(ast4[i4], 16 * (len(tiles) // 4))

    def qkv_phase(name, col_base_tile, rhs_scr, n_grp, dst_kind):
        dov = dst_kind == "v"
        with ExitStack() as ctx:
            sb = lambda n, sh, dt: ctx.enter_context(nc.sbuf_tensor(n, sh, dt))
            Wsb = sb(f"{name}_W", [128, 16 * 2048], BF16)
            Wf32 = sb(f"{name}_Wf", [128, 2 * 2048], F32)
            Xg = sb(f"{name}_X", [128, 16 * 2048], BF16)
            stg = sb(f"{name}_st", [128, 2 * 2048], BF16)
            bia = sb(f"{name}_b", [128, 48], F32)
            wld = ctx.enter_context(nc.semaphore(f"{name}_wld"))
            xld = ctx.enter_context(nc.semaphore(f"{name}_xld"))
            mm = ctx.enter_context(nc.semaphore(f"{name}_mm"))
            ev = ctx.enter_context(nc.semaphore(f"{name}_ev"))
            st2 = [ctx.enter_context(nc.semaphore(f"{name}_st{i}")) for i in range(2)]
            if dov:
                stg2 = sb(f"{name}_s2", [128, 2 * 2048], BF16)
                idn = sb(f"{name}_id", [128, 128], BF16)
                tt = ctx.enter_context(nc.semaphore(f"{name}_tt"))
                vv = ctx.enter_context(nc.semaphore(f"{name}_vv"))
                psMa = ctx.enter_context(nc.psum_tensor(f"{name}_pa", [128, 2048], F32))
                pspt = ctx.enter_context(nc.psum_tensor(f"{name}_pt", [128, 2048], BF16))
                psM = [psMa, psMa]
            else:
                psMa = ctx.enter_context(nc.psum_tensor(f"{name}_pa", [128, 2048], F32))
                psMb = ctx.enter_context(nc.psum_tensor(f"{name}_pb", [128, 2048], F32))
                psM = [psMa, psMb]
            blk = ctx.enter_context(nc.Block())
            NK = n_grp * 16

            wcv = ctx.enter_context(nc.semaphore(f"{name}_wcv"))
            wl2 = [ctx.enter_context(nc.semaphore(f"{name}_wl{i}")) for i in range(2)]

            @blk.gpsimd
            def _(g):
                c0 = col_base_tile * 128
                for hh in range(16):
                    if hh >= 2:
                        g.wait_ge(wcv, hh - 1)
                    g.dma_start(Wf32[:, (hh % 2) * 2048:(hh % 2 + 1) * 2048],
                                qkv_w[hh * 128:(hh + 1) * 128, c0:c0 + 2048]
                                ).then_inc(wl2[hh % 2], 16)
                g.dma_start(bia[:, :], qkv_bp[:, :]).then_inc(wld, 16)
                if dov:
                    g.dma_start(idn[:, :], ident_in[:, :]).then_inc(wld, 16)

            @blk.sync
            def _(sy):
                nst = 0
                for grp in range(n_grp):
                    if grp >= 1:
                        sy.wait_ge(mm, 16 * grp)
                    for hh in range(16):
                        for tc in range(4):
                            r0 = grp * 2048 + tc * 512
                            sy.dma_start(
                                Xg[:, hh * 2048 + tc * 512: hh * 2048 + (tc + 1) * 512],
                                rhs_scr[r0:r0 + 512, hh * 128:(hh + 1) * 128],
                                transpose=True).then_inc(xld, 16)
                    for j in range(16):
                        k = grp * 16 + j
                        if dov:
                            sy.wait_ge(vv, 2 * (k + 1))
                            sy.dma_start(v_s[j, :, grp * 2048:(grp + 1) * 2048],
                                         stg2[:, (k % 2) * 2048:(k % 2 + 1) * 2048]
                                         ).then_inc(st2[k % 2], 16)
                        else:
                            sy.wait_ge(ev, k + 1)
                            src = stg[:, (k % 2) * 2048:(k % 2 + 1) * 2048]
                            if dst_kind == "k":
                                sy.dma_start(k_s[j, :, grp * 2048:(grp + 1) * 2048],
                                             src).then_inc(st2[k % 2], 16)
                            else:
                                sy.dma_start(q_s[j, :, :], src).then_inc(st2[k % 2], 16)
                        nst += 1
                for i2 in range(2):
                    sy.wait_ge(st2[i2], 16 * (nst // 2))

            @blk.tensor
            def _(t):
                t.wait_ge(wld, 16 * (2 if dov else 1))
                t.wait_ge(wcv, 16)
                for grp in range(n_grp):
                    t.wait_ge(xld, 16 * 64 * (grp + 1))
                    for j in range(16):
                        k = grp * 16 + j
                        ps = psM[k % 2]
                        if dov:
                            if k >= 1:
                                t.wait_ge(ev, k)
                        elif k >= 2:
                            t.wait_ge(ev, k - 1)
                        for hh in range(16):
                            for tc in range(4):
                                ins = t.matmul(
                                    ps[:, tc * 512:(tc + 1) * 512],
                                    Wsb[:, hh * 2048 + j * 128: hh * 2048 + (j + 1) * 128],
                                    Xg[:, hh * 2048 + tc * 512: hh * 2048 + (tc + 1) * 512],
                                    start=(hh == 0), stop=(hh == 15))
                        ins.then_inc(mm, 1)
                        if dov:
                            t.wait_ge(ev, k + 1)
                            for u in range(16):
                                if u in (0, 8) and k * 2 + u // 8 >= 2:
                                    t.wait_ge(vv, k * 2 + u // 8 - 1)
                                ins2 = t.transpose(
                                    pspt[:, (u // 8) * 1024 + (u % 8) * 128:
                                         (u // 8) * 1024 + (u % 8 + 1) * 128],
                                    stg[:, (k % 2) * 2048 + u * 128:
                                        (k % 2) * 2048 + (u + 1) * 128],
                                    idn[:, :])
                                if u in (7, 15):
                                    ins2.then_inc(tt, 1)

            @blk.scalar
            def _(s):
                for grp in range(n_grp):
                    for j in range(16):
                        k = grp * 16 + j
                        s.wait_ge(mm, k + 1)
                        if k >= 2:
                            s.wait_ge(st2[k % 2], 16 * (k // 2))
                        ps = psM[k % 2]
                        for tc in range(4):
                            ins = s.activation(
                                stg[:, (k % 2) * 2048 + tc * 512:
                                    (k % 2) * 2048 + (tc + 1) * 512],
                                ps[:, tc * 512:(tc + 1) * 512], AF.Identity,
                                bias=bia[:, col_base_tile + j: col_base_tile + j + 1])
                        ins.then_inc(ev, 1)

            @blk.vector
            def _(v):
                for hh in range(16):
                    v.wait_ge(wl2[hh % 2], 16 * (hh // 2 + 1))
                    v.drain()
                    v.tensor_copy(Wsb[:, hh * 2048:(hh + 1) * 2048],
                                  Wf32[:, (hh % 2) * 2048:(hh % 2 + 1) * 2048]
                                  ).then_inc(wcv, 1)
                if not dov:
                    return
                for k in range(NK):
                    for half in range(2):
                        v.wait_ge(tt, 2 * k + half + 1)
                        if k >= 2:
                            v.wait_ge(st2[k % 2], 16 * (k // 2))
                        v.tensor_copy(
                            stg2[:, (k % 2) * 2048 + half * 1024:
                                 (k % 2) * 2048 + (half + 1) * 1024],
                            pspt[:, half * 1024:(half + 1) * 1024]
                        ).then_inc(vv, 1)

    qkv_phase("A2K", 16, ln1x_s, 2, "k")
    qkv_phase("A2V", 32, ln1x_s, 2, "v")
    qkv_phase("A2Q", 0, ln1xo_s, 1, "q")

    with ExitStack() as ctx:
        sb = lambda n, sh, dt: ctx.enter_context(nc.sbuf_tensor(n, sh, dt))
        K2 = sb("b_K", [128, 2 * S], BF16)
        V2 = sb("b_V", [128, 2 * S], BF16)
        Q2 = sb("b_Q", [128, 2 * TOWN], BF16)
        AL = sb("b_al", [2, 2 * S], BF16)
        ON2 = sb("b_on", [2, 128], BF16)
        MSK = sb("b_msk", [128, 16 * 1024], BF16)
        UM = sb("b_um", [128, 128], BF16)
        PB = sb("b_pb", [128, 8 * S], BF16)
        PT = sb("b_ptb", [128, 32 * 512], BF16)
        CTXs = sb("b_ctx", [128, 4 * 512], BF16)
        STT = sb("b_stt", [128, 4 * 32], F32)
        IDN = sb("b_id", [128, 128], BF16)
        hld2 = [ctx.enter_context(nc.semaphore(f"b_hld{i}")) for i in range(2)]
        mld = ctx.enter_context(nc.semaphore("b_mld"))
        mm = ctx.enter_context(nc.semaphore("b_mm"))
        tmm = ctx.enter_context(nc.semaphore("b_tmm"))
        pvm = ctx.enter_context(nc.semaphore("b_pvm"))
        vmx = ctx.enter_context(nc.semaphore("b_vmx"))
        vfin = ctx.enter_context(nc.semaphore("b_vfin"))
        vdif = ctx.enter_context(nc.semaphore("b_vdif"))
        ptv = ctx.enter_context(nc.semaphore("b_ptv"))
        ce = ctx.enter_context(nc.semaphore("b_ce"))
        sex = ctx.enter_context(nc.semaphore("b_sex"))
        sco = ctx.enter_context(nc.semaphore("b_sco"))
        hd = ctx.enter_context(nc.semaphore("b_hd"))
        vsc = ctx.enter_context(nc.semaphore("b_vsc"))
        cst4 = [ctx.enter_context(nc.semaphore(f"b_cst{i}")) for i in range(4)]
        psA = ctx.enter_context(nc.psum_tensor("b_psA", [128, 1024], F32))
        psB = ctx.enter_context(nc.psum_tensor("b_psB", [128, 1024], F32))
        psC = ctx.enter_context(nc.psum_tensor("b_psC", [128, 1024], F32))
        pspt = ctx.enter_context(nc.psum_tensor("b_pspt", [128, 2048], BF16))
        blk = ctx.enter_context(nc.Block())
        psS = [psA, psB, psC]
        psctx = pspt.bitcast(F32)[:, 512:1024]

        @blk.sync
        def _(sy):
            sy.dma_start(MSK[:, :], masks_in[:, :]).then_inc(mld, 16)
            sy.dma_start(IDN[:, :], ident_in[:, :]).then_inc(mld, 16)
            sy.dma_start(ON2[:, :], ones2_in[:, :]).then_inc(mld, 16)
            sy.dma_start(UM[:, :], msku_in[:, :]).then_inc(mld, 16)
            for h in range(NH):
                par = h % 2
                if h >= 2:
                    sy.wait_ge(hd, h - 1)
                sy.dma_start(K2[:, par * S:(par + 1) * S], k_s[h, :, :]).then_inc(hld2[par], 16)
                sy.dma_start(V2[:, par * S:(par + 1) * S], v_s[h, :, :]).then_inc(hld2[par], 16)
                sy.dma_start(Q2[:, par * TOWN:(par + 1) * TOWN],
                             q_s[h, :, :]).then_inc(hld2[par], 16)
                sy.dma_start(AL[:, par * S:(par + 1) * S],
                             alibi_in[h, :, :]).then_inc(hld2[par], 16)
                if h >= 1:
                    for s_ in range(4):
                        gsp = (h - 1) * 4 + s_
                        sy.wait_ge(ce, gsp + 1)
                        sy.dma_start(ctx_s[h - 1, :, s_ * 512:(s_ + 1) * 512],
                                     CTXs[:, (gsp % 4) * 512:(gsp % 4 + 1) * 512]
                                     ).then_inc(cst4[gsp % 4], 16)
            for s_ in range(4):
                gsp = 15 * 4 + s_
                sy.wait_ge(ce, gsp + 1)
                sy.dma_start(ctx_s[15, :, s_ * 512:(s_ + 1) * 512],
                             CTXs[:, (gsp % 4) * 512:(gsp % 4 + 1) * 512]
                             ).then_inc(cst4[gsp % 4], 16)
            for i4 in range(4):
                sy.wait_ge(cst4[i4], 16 * 16)

        @blk.tensor
        def _(t):
            t.wait_ge(mld, 64)
            pc = tg = 0
            slots = [(h, s) for h in range(NH) for s in range(4)]

            def emit_phases(gs):
                nonlocal pc
                h, s_ = slots[gs]
                par = h % 2
                nP = NPH[s_]
                pb0 = (gs % 2) * 4 * S
                if s_ == 0:
                    t.wait_ge(hld2[par], 64 * (h // 2 + 1))
                for r_ in range(4):
                    tile = s_ * 4 + r_
                    qT = Q2[:, par * TOWN + tile * 128: par * TOWN + (tile + 1) * 128]
                    for p_ in range(nP):
                        k0 = p_ * 1024
                        ps = psS[pc % 3]
                        if pc >= 3:
                            t.wait_ge(sex, pc - 2)
                        for c in range(2):
                            t.matmul(ps[:, c * 512:(c + 1) * 512], ON2[:, :],
                                     AL[:, par * S + k0 + c * 512:
                                        par * S + k0 + (c + 1) * 512],
                                     start=True, stop=False)
                        if p_ == nP - 1:
                            for c in range(2):
                                t.matmul(ps[:, c * 512:(c + 1) * 512], UM[:, :],
                                         MSK[:, tile * 1024 + c * 512:
                                             tile * 1024 + (c + 1) * 512],
                                         start=False, stop=False)
                        for c in range(2):
                            ins = t.matmul(
                                ps[:, c * 512:(c + 1) * 512], qT,
                                K2[:, par * S + k0 + c * 512:
                                   par * S + k0 + (c + 1) * 512],
                                start=False, stop=True)
                        ins.then_inc(mm, 1)
                        pc += 1

            def emit_tail(gs):
                nonlocal tg
                h, s_ = slots[gs]
                par = h % 2
                nP = NPH[s_]
                pb0 = (gs % 2) * 4 * S
                t.wait_ge(vfin, gs * 4 + 4)
                for tk in range(8 * nP):
                    if tg >= 2:
                        t.wait_ge(ptv, tg - 1)
                    if tk == 1 and gs >= 1:
                        t.wait_ge(ce, gs)
                    for r_ in range(4):
                        ins = t.transpose(
                            pspt[:, (tg % 2) * 1024 + r_ * 128:
                                 (tg % 2) * 1024 + (r_ + 1) * 128],
                            PB[:, pb0 + r_ * S + tk * 128: pb0 + r_ * S + (tk + 1) * 128],
                            IDN[:, :])
                    ins.then_inc(tmm, 1)
                    tg += 1
                t.wait_ge(ptv, tg)
                if gs >= 1:
                    t.wait_ge(ce, gs)
                for tk in range(8 * nP):
                    ins = t.matmul(psctx[:, :],
                                   V2[:, par * S + tk * 128: par * S + (tk + 1) * 128],
                                   PT[:, tk * 512:(tk + 1) * 512],
                                   start=(tk == 0), stop=(tk == 8 * nP - 1))
                ins.then_inc(pvm, 1)
                if s_ == 3:
                    t.drain().then_inc(hd, 1)

            for gs in range(len(slots)):
                emit_phases(gs)
                if gs >= 1:
                    emit_tail(gs - 1)
            emit_tail(len(slots) - 1)

        @blk.vector
        def _(v):
            v.wait_ge(mld, 64)
            pc = tg = 0
            scoc = 0
            slots = [(h, s) for h in range(NH) for s in range(4)]

            def v_maxes(gs):
                nonlocal pc
                h, s_ = slots[gs]
                nP = NPH[s_]
                sex_at = []
                for r_ in range(4):
                    tile = s_ * 4 + r_
                    rb = r_ * 32
                    for p_ in range(nP):
                        v.wait_ge(mm, pc + 1)
                        ps = psS[pc % 3]
                        v.tensor_reduce(STT[:, rb + p_: rb + p_ + 1], ps[:, :],
                                        AX.X, ALU.max, negate=True).then_inc(vmx, 1)
                        pc += 1
                    sex_at.append(pc)
                return sex_at

            def sv(col, inner, bcast=False):
                return bass.AP(STT, col, [[128, 128], [32, 4],
                                          [0 if bcast else 1, inner]])

            def v_finals(gs, sex_at):
                nonlocal scoc
                h, s_ = slots[gs]
                nP = NPH[s_]
                pb0 = (gs % 2) * 4 * S
                v.wait_ge(sex, sex_at[3])
                if nP == 1:
                    v.reciprocal(sv(19, 1), sv(4, 1))
                    v.drain()
                else:
                    v.tensor_reduce(sv(8, 1), sv(0, nP), AX.X, ALU.min)
                    v.drain()
                    v.tensor_tensor(sv(10, nP), sv(8, nP, bcast=True),
                                    sv(0, nP), ALU.subtract).then_inc(vdif, 1)
                    scoc += 1
                    v.wait_ge(sco, scoc)
                    v.tensor_tensor(sv(20, nP), sv(14, nP), sv(4, nP), ALU.mult)
                    v.drain()
                    v.tensor_reduce(sv(18, 1), sv(20, nP), AX.X, ALU.add)
                    v.drain()
                    v.reciprocal(sv(19, 1), sv(18, 1))
                    v.drain()
                    v.tensor_tensor(sv(14, nP), sv(14, nP),
                                    sv(19, nP, bcast=True), ALU.mult)
                    v.drain()
                for r_ in range(4):
                    rb = r_ * 32
                    for p_ in range(nP):
                        ins = v.tensor_scalar(
                            PB[:, pb0 + r_ * S + p_ * 1024:
                               pb0 + r_ * S + (p_ + 1) * 1024],
                            PB[:, pb0 + r_ * S + p_ * 1024:
                               pb0 + r_ * S + (p_ + 1) * 1024],
                            STT[:, rb + (19 if nP == 1 else 14 + p_):
                                rb + (20 if nP == 1 else 15 + p_)],
                            None, ALU.mult)
                    ins.then_inc(vfin, 1)

            def v_evacs(gs):
                nonlocal tg
                h, s_ = slots[gs]
                nP = NPH[s_]
                for tk in range(8 * nP):
                    v.wait_ge(tmm, tg + 1)
                    v.tensor_copy(PT[:, tk * 512:(tk + 1) * 512],
                                  pspt[:, (tg % 2) * 1024:
                                       (tg % 2) * 1024 + 512]).then_inc(ptv, 1)
                    tg += 1

            def v_ctx(gs):
                v.wait_ge(pvm, gs + 1)
                if gs >= 4:
                    v.wait_ge(cst4[gs % 4], 16 * (gs // 4))
                v.tensor_copy(CTXs[:, (gs % 4) * 512:(gs % 4 + 1) * 512],
                              psctx[:, :]).then_inc(ce, 1)

            for gs in range(len(slots)):
                sx_at = v_maxes(gs)
                if gs >= 1:
                    v_evacs(gs - 1)
                v_finals(gs, sx_at)
                if gs >= 1:
                    v_ctx(gs - 1)
            v_evacs(len(slots) - 1)
            v_ctx(len(slots) - 1)

        @blk.scalar
        def _(s):
            pc = 0
            scoc = 0
            for h in range(NH):
                for s_ in range(4):
                    nP = NPH[s_]
                    pb0 = ((h * 4 + s_) % 2) * 4 * S
                    for r_ in range(4):
                        rb = r_ * 32
                        for p_ in range(nP):
                            s.wait_ge(vmx, pc + 1)
                            ps = psS[pc % 3]
                            s.activation(PB[:, pb0 + r_ * S + p_ * 1024:
                                            pb0 + r_ * S + (p_ + 1) * 1024],
                                         ps[:, :], AF.Exp,
                                         bias=STT[:, rb + p_: rb + p_ + 1],
                                         accum_out=STT[:, rb + 4 + p_: rb + 5 + p_]
                                         ).then_inc(sex, 1)
                            pc += 1
                    if nP > 1:
                        scoc += 1
                        s.wait_ge(vdif, scoc)
                        s.activation(bass.AP(STT, 14, [[128, 128], [32, 4], [1, nP]]),
                                     bass.AP(STT, 10, [[128, 128], [32, 4], [1, nP]]),
                                     AF.Exp).then_inc(sco, 1)

    with ExitStack() as ctx:
        sb = lambda n, sh, dt: ctx.enter_context(nc.sbuf_tensor(n, sh, dt))
        CTXf = sb("c_ctx", [128, 16 * 2048], BF16)
        OW = sb("c_ow", [128, 16 * 2048], BF16)
        OWf = sb("c_owf", [128, 2 * 2048], F32)
        XO = sb("c_xo", [128, 2 * 2048], F32)
        ATT = sb("c_att", [128, 2048], F32)
        LN2A = sb("c_l2", [128, 2048], F32)
        X2T = sb("c_x2", [128, 2048], F32)
        LN3X = sb("c_l3", [128, 2 * 2048], BF16)
        OB2 = sb("c_ob", [2, 2048], BF16)
        ON2c = sb("c_on", [2, 128], BF16)
        jnkC = sb("c_jnk", [128, 2048], BF16)
        st8 = sb("c_st", [128, 8], F32)
        cld2 = [ctx.enter_context(nc.semaphore(f"c_ld{i}")) for i in range(2)]
        wldc = ctx.enter_context(nc.semaphore("c_wld"))
        gwl = ctx.enter_context(nc.semaphore("c_gwl"))
        owc = ctx.enter_context(nc.semaphore("c_owc"))
        owl2 = [ctx.enter_context(nc.semaphore(f"c_owl{i}")) for i in range(2)]
        mmc = ctx.enter_context(nc.semaphore("c_mm"))
        cs = ctx.enter_context(nc.semaphore("c_cs"))
        cv = ctx.enter_context(nc.semaphore("c_cv"))
        xst = ctx.enter_context(nc.semaphore("c_xst"))
        lst2 = [ctx.enter_context(nc.semaphore(f"c_lst{i}")) for i in range(2)]
        ps4a = ctx.enter_context(nc.psum_tensor("c_pa", [128, 2048], F32))
        ps4b = ctx.enter_context(nc.psum_tensor("c_pb", [128, 2048], F32))
        blk = ctx.enter_context(nc.Block())
        ps4 = [ps4a, ps4b]

        @blk.gpsimd
        def _(g):
            for cf in range(16):
                if cf >= 2:
                    g.wait_ge(owc, cf - 1)
                g.dma_start(OWf[:, (cf % 2) * 2048:(cf % 2 + 1) * 2048],
                            out_w[cf * 128:(cf + 1) * 128, :]).then_inc(owl2[cf % 2], 16)
            g.dma_start(OB2[:, :], out_b2[:, :]).then_inc(gwl, 16)

        @blk.sync
        def _(sy):
            for hh in range(16):
                sy.dma_start(CTXf[:, hh * 2048:(hh + 1) * 2048],
                             ctx_s[hh, :, :]).then_inc(wldc, 16)
            sy.dma_start(ON2c[:, :], ones2_in[:, :]).then_inc(wldc, 16)
            for tt in range(17):
                if tt < 16:
                    if tt >= 2:
                        sy.wait_ge(cv, 7 * (tt - 2) + 4)
                    sy.dma_start(XO[:, (tt % 2) * 2048:((tt % 2) + 1) * 2048],
                                 x_own[tt * 128:(tt + 1) * 128, :]).then_inc(cld2[tt % 2], 16)
                if tt >= 1:
                    tp = tt - 1
                    sy.wait_ge(cv, 7 * tp + 4)
                    sy.dma_start(x2_s[tp * 128:(tp + 1) * 128, :],
                                 X2T[:, :]).then_inc(xst, 16)
                    sy.wait_ge(cs, 7 * tp + 7)
                    sy.dma_start(ln3x_s[tp * 128:(tp + 1) * 128, :],
                                 LN3X[:, (tp % 2) * 2048:((tp % 2) + 1) * 2048]
                                 ).then_inc(lst2[tp % 2], 16)
            sy.wait_ge(xst, 16 * 16)
            for i2 in range(2):
                sy.wait_ge(lst2[i2], 16 * 8)

        @blk.tensor
        def _(t):
            t.wait_ge(wldc, 16 * 17)
            t.wait_ge(gwl, 16 * 1)
            t.wait_ge(owc, 16)
            for tt in range(16):
                ps = ps4[tt % 2]
                if tt >= 2:
                    t.wait_ge(cs, 7 * (tt - 2) + 1)
                for ho in range(4):
                    t.matmul(ps[:, ho * 512:(ho + 1) * 512], ON2c[:, :],
                             OB2[:, ho * 512:(ho + 1) * 512], start=True, stop=False)
                for cf in range(16):
                    for ho in range(4):
                        ins = t.matmul(
                            ps[:, ho * 512:(ho + 1) * 512],
                            CTXf[:, cf * 2048 + tt * 128: cf * 2048 + (tt + 1) * 128],
                            OW[:, cf * 2048 + ho * 512: cf * 2048 + (ho + 1) * 512],
                            start=False, stop=(cf == 15))
                ins.then_inc(mmc, 1)

        @blk.scalar
        def _(s):
            for tt in range(16):
                ps = ps4[tt % 2]
                cs0, cv0 = 7 * tt, 7 * tt
                s.wait_ge(mmc, tt + 1)
                if tt >= 1:
                    s.wait_ge(cv, 7 * (tt - 1) + 1)
                s.drain()
                for ho in range(4):
                    ins = s.activation(ATT[:, ho * 512:(ho + 1) * 512],
                                       ps[:, ho * 512:(ho + 1) * 512], AF.Copy)
                ins.then_inc(cs, 1)
                s.drain()
                s.activation(jnkC[:, :], ATT[:, :], AF.Square,
                             accum_out=st8[:, 1:2]).then_inc(cs, 1)
                s.wait_ge(cv, cv0 + 2)
                s.activation(st8[:, 5:6], st8[:, 4:5],
                             AF.Sqrt).then_inc(cs, 1)
                s.wait_ge(cv, cv0 + 3)
                s.activation(LN2A[:, :], ATT[:, :], AF.Identity,
                             bias=st8[:, 7:8], scale=st8[:, 6:7]).then_inc(cs, 1)
                s.wait_ge(cv, cv0 + 4)
                s.drain()
                s.activation(jnkC[:, :], X2T[:, :], AF.Square,
                             accum_out=st8[:, 1:2]).then_inc(cs, 1)
                s.wait_ge(cv, cv0 + 6)
                s.activation(st8[:, 5:6], st8[:, 4:5],
                             AF.Sqrt).then_inc(cs, 1)
                s.wait_ge(cv, cv0 + 7)
                if tt >= 2:
                    s.wait_ge(lst2[tt % 2], 16 * (tt // 2))
                s.activation(LN3X[:, (tt % 2) * 2048:((tt % 2) + 1) * 2048],
                             X2T[:, :], AF.Identity,
                             bias=st8[:, 7:8], scale=st8[:, 6:7]).then_inc(cs, 1)

        @blk.vector
        def _(v):
            for cf in range(16):
                v.wait_ge(owl2[cf % 2], 16 * (cf // 2 + 1))
                v.drain()
                v.tensor_copy(OW[:, cf * 2048:(cf + 1) * 2048],
                              OWf[:, (cf % 2) * 2048:(cf % 2 + 1) * 2048]
                              ).then_inc(owc, 1)
            for tt in range(16):
                cs0, cv0 = 7 * tt, 7 * tt
                v.wait_ge(cs, cs0 + 1)
                v.drain()
                v.tensor_reduce(st8[:, 0:1], ATT[:, :], AX.X, ALU.add,
                                negate=True).then_inc(cv, 1)
                v.wait_ge(cs, cs0 + 2)
                v.drain()
                v.tensor_scalar(st8[:, 2:3], st8[:, 0:1], 1.0 / H, None, ALU.mult)
                v.tensor_scalar(st8[:, 4:5], st8[:, 1:2], 1.0 / H, EPS, ALU.mult,
                                op1=ALU.add)
                v.drain()
                v.tensor_tensor(st8[:, 3:4], st8[:, 2:3], st8[:, 2:3], ALU.mult)
                v.drain()
                v.tensor_tensor(st8[:, 4:5], st8[:, 4:5], st8[:, 3:4],
                                ALU.subtract).then_inc(cv, 1)
                v.wait_ge(cs, cs0 + 3)
                v.drain()
                v.reciprocal(st8[:, 6:7], st8[:, 5:6])
                v.drain()
                v.tensor_tensor(st8[:, 7:8], st8[:, 2:3], st8[:, 6:7],
                                ALU.mult).then_inc(cv, 1)
                v.wait_ge(cs, cs0 + 4)
                v.wait_ge(cld2[tt % 2], 16 * (tt // 2 + 1))
                if tt >= 1:
                    v.wait_ge(xst, 16 * tt)
                    v.wait_ge(cs, 7 * (tt - 1) + 7)
                v.drain()
                v.tensor_tensor(X2T[:, :], XO[:, (tt % 2) * 2048:((tt % 2) + 1) * 2048],
                                LN2A[:, :], ALU.add).then_inc(cv, 1)
                v.drain()
                v.tensor_reduce(st8[:, 0:1], X2T[:, :], AX.X, ALU.add,
                                negate=True).then_inc(cv, 1)
                v.wait_ge(cs, cs0 + 5)
                v.drain()
                v.tensor_scalar(st8[:, 2:3], st8[:, 0:1], 1.0 / H, None, ALU.mult)
                v.tensor_scalar(st8[:, 4:5], st8[:, 1:2], 1.0 / H, EPS, ALU.mult,
                                op1=ALU.add)
                v.drain()
                v.tensor_tensor(st8[:, 3:4], st8[:, 2:3], st8[:, 2:3], ALU.mult)
                v.drain()
                v.tensor_tensor(st8[:, 4:5], st8[:, 4:5], st8[:, 3:4],
                                ALU.subtract).then_inc(cv, 1)
                v.wait_ge(cs, cs0 + 6)
                v.drain()
                v.reciprocal(st8[:, 6:7], st8[:, 5:6])
                v.drain()
                v.tensor_tensor(st8[:, 7:8], st8[:, 2:3], st8[:, 6:7],
                                ALU.mult).then_inc(cv, 1)

    with ExitStack() as ctx:
        sb = lambda n, sh, dt: ctx.enter_context(nc.sbuf_tensor(n, sh, dt))
        L3T = sb("d_l3", [128, 16 * 512], BF16)
        WG = sb("d_wg", [128, 2 * 8192], BF16)
        WU = sb("d_wu", [128, 2 * 8192], BF16)
        HA = sb("d_ha", [128, 64 * 512], BF16)
        WD = sb("d_wd", [128, 16 * 512], BF16)
        DFT = sb("d_dft", [128, 16 * 512], BF16)
        DTT = sb("d_dtt", [128, 2048], BF16)
        X2R = sb("d_x2", [128, 2048], F32)
        OUTT = sb("d_out", [128, 2048], F32)
        LN4A = sb("d_l4", [128, 2048], BF16)
        GACT = sb("d_ga", [128, 512], BF16)
        UACT = sb("d_ua", [128, 512], BF16)
        NCSh = sb("d_ncsh", [1, 512], BF16)
        NCSl = sb("d_ncsl", [1, 512], BF16)
        NDh, NDl = NCSh, NCSl
        ON2d = sb("d_on2", [2, 128], BF16)
        ONC = sb("d_onc", [128, 1], BF16)
        SGs = sb("d_sg", [128, 64], F32)
        SUs = sb("d_su", [128, 64], F32)
        SDs = sb("d_sd", [128, 16], F32)
        IDNd = sb("d_id", [128, 128], BF16)
        st8d = sb("d_st", [128, 8], F32)
        l3l = ctx.enter_context(nc.semaphore("d_l3l"))
        wgl2 = [ctx.enter_context(nc.semaphore(f"d_wgl{i}")) for i in range(2)]
        dwl2 = [ctx.enter_context(nc.semaphore(f"d_dwl{i}")) for i in range(2)]
        xl2 = [ctx.enter_context(nc.semaphore(f"d_xl{i}")) for i in range(2)]
        gm = ctx.enter_context(nc.semaphore("d_gm"))
        csm = ctx.enter_context(nc.semaphore("d_csm"))
        dcs = ctx.enter_context(nc.semaphore("d_dcs"))
        sgd = ctx.enter_context(nc.semaphore("d_sgd"))
        hv = ctx.enter_context(nc.semaphore("d_hv"))
        dmi = ctx.enter_context(nc.semaphore("d_dmi"))
        dm = ctx.enter_context(nc.semaphore("d_dm"))
        dv = ctx.enter_context(nc.semaphore("d_dv"))
        tm = ctx.enter_context(nc.semaphore("d_tm"))
        vE = ctx.enter_context(nc.semaphore("d_vE"))
        sv4 = ctx.enter_context(nc.semaphore("d_sv4"))
        ss4 = ctx.enter_context(nc.semaphore("d_ss4"))
        fo = ctx.enter_context(nc.semaphore("d_fo"))
        mldd = ctx.enter_context(nc.semaphore("d_mld"))
        psGU = ctx.enter_context(nc.psum_tensor("d_pgu", [128, 2048], F32))
        psD = psGU
        psE = ctx.enter_context(nc.psum_tensor("d_pe", [128, 2048], BF16))
        blk = ctx.enter_context(nc.Block())

        def ap3(tensor, off, dims):
            return bass.AP(tensor, off, dims)

        @blk.gpsimd
        def _(g):
            g.dma_start(SGs[:, :], sg_p[:, :]).then_inc(mldd, 16)
            g.dma_start(SUs[:, :], su_p[:, :]).then_inc(mldd, 16)
            g.dma_start(SDs[:, :], sd_p[:, :]).then_inc(mldd, 16)
            g.dma_start(IDNd[:, :], ident_in[:, :]).then_inc(mldd, 16)
            g.dma_start(ON2d[:, :], ones2_in[:, :]).then_inc(mldd, 16)
            g.dma_start(ONC[:, :], onesc_in[:, :]).then_inc(mldd, 16)
            sxg = 0
            ig = 0
            for tc in range(4):
                for sx in range(16):
                    if sxg >= 2:
                        g.wait_ge(gm, 8 * (sxg - 1))
                    for (scr, dstt) in ((gate_u8, WG), (up_u8, WU)):
                        in_ap = bass.AP(scr, sx * 512,
                                        [[IFF, 128], [128 * IFF, 16], [1, 512]])
                        out_ap = bass.AP(dstt, (sxg % 2) * 8192,
                                         [[2 * 8192, 128], [512, 16], [1, 512]])
                        g.dma_start(out_ap, in_ap).then_inc(wgl2[sxg % 2], 16)
                    sxg += 1
                for q in range(4):
                    for ib in range(8):
                        if ig >= 2:
                            g.wait_ge(dmi, ig - 1)
                        for ii in range(8):
                            i = ib * 8 + ii
                            g.dma_start(
                                WD[:, ((ig % 2) * 8 + ii) * 512:
                                   ((ig % 2) * 8 + ii + 1) * 512],
                                down_u8[i * 128:(i + 1) * 128, q * 512:(q + 1) * 512]
                            ).then_inc(dwl2[ig % 2], 16)
                        ig += 1

        @blk.sync
        def _(sy):
            tsg = 0
            for tc in range(4):
                if tc >= 1:
                    sy.wait_ge(gm, 128 * tc)
                for hh in range(16):
                    sy.dma_start(L3T[:, hh * 512:(hh + 1) * 512],
                                 ln3x_s[tc * 512:(tc + 1) * 512, hh * 128:(hh + 1) * 128],
                                 transpose=True).then_inc(l3l, 16)
                for ts_ in range(4):
                    if tsg >= 1:
                        sy.wait_ge(sv4, 4 * (tsg - 1) + 4)
                    sy.dma_start(X2R[:, :],
                                 x2_s[tsg * 128:(tsg + 1) * 128, :]).then_inc(xl2[tsg % 2], 16)
                    sy.wait_ge(sv4, 4 * tsg + 4)
                    sy.dma_start(out_d[tsg * 128:(tsg + 1) * 128, :],
                                 OUTT[:, :]).then_inc(fo, 16)
                    tsg += 1
            sy.wait_ge(fo, 16 * 16)

        @blk.tensor
        def _(t):
            t.wait_ge(mldd, 96)
            igl = 0
            qg = 0
            idg = 0
            tsg = 0
            for tc in range(4):
                t.wait_ge(l3l, 256 * (tc + 1))
                if tc >= 1:
                    t.wait_ge(dv, 16 * tc)
                for hh in range(16):
                    ins = t.matmul(psD[0:1, 0:512], ONC[:, :],
                                   L3T[:, hh * 512:(hh + 1) * 512],
                                   start=(hh == 0), stop=(hh == 15))
                ins.then_inc(csm, 1)
                t.wait_ge(dcs, 2 * tc + 1)
                for sx in range(16):
                    sxg = tc * 16 + sx
                    t.wait_ge(wgl2[sxg % 2], 32 * (sxg // 2 + 1))
                    for i in range(4):
                        pp = igl % 2
                        psG = psGU[:, (2 * pp) * 512:(2 * pp + 1) * 512]
                        psU = psGU[:, (2 * pp + 1) * 512:(2 * pp + 2) * 512]
                        if igl >= 2:
                            t.wait_ge(sgd, igl - 1)
                            t.wait_ge(hv, igl - 1)
                        t.matmul(psG, ON2d[0:1, :], NCSh[0:1, :],
                                 start=True, stop=False)
                        t.matmul(psG, ON2d[0:1, :], NCSl[0:1, :],
                                 start=False, stop=False)
                        for hh in range(16):
                            ins = t.matmul(psG,
                                           WG[:, (sxg % 2) * 8192 + hh * 512 + i * 128:
                                              (sxg % 2) * 8192 + hh * 512 + (i + 1) * 128],
                                           L3T[:, hh * 512:(hh + 1) * 512],
                                           start=False, stop=(hh == 15))
                        ins.then_inc(gm, 1)
                        t.matmul(psU, ON2d[0:1, :], NCSh[0:1, :],
                                 start=True, stop=False)
                        t.matmul(psU, ON2d[0:1, :], NCSl[0:1, :],
                                 start=False, stop=False)
                        for hh in range(16):
                            ins = t.matmul(psU,
                                           WU[:, (sxg % 2) * 8192 + hh * 512 + i * 128:
                                              (sxg % 2) * 8192 + hh * 512 + (i + 1) * 128],
                                           L3T[:, hh * 512:(hh + 1) * 512],
                                           start=False, stop=(hh == 15))
                        ins.then_inc(gm, 1)
                        igl += 1
                t.wait_ge(hv, 64 * (tc + 1))
                for ic in range(64):
                    ins = t.matmul(psD[0:1, 0:512], ONC[:, :],
                                   HA[:, ic * 512:(ic + 1) * 512],
                                   start=(ic == 0), stop=(ic == 63))
                ins.then_inc(csm, 1)
                t.wait_ge(dcs, 2 * tc + 2)
                for q in range(4):
                    if qg >= 1:
                        t.wait_ge(dv, 4 * qg)
                    for hm in range(4):
                        t.matmul(psD[:, hm * 512:(hm + 1) * 512], ON2d[0:1, :],
                                 NDh[0:1, :], start=True, stop=False)
                        t.matmul(psD[:, hm * 512:(hm + 1) * 512], ON2d[0:1, :],
                                 NDl[0:1, :], start=False, stop=False)
                    for ib in range(8):
                        t.wait_ge(dwl2[idg % 2], 16 * 8 * (idg // 2 + 1))
                        for hm in range(4):
                            for ii in range(8):
                                i = ib * 8 + ii
                                ins = t.matmul(
                                    psD[:, hm * 512:(hm + 1) * 512],
                                    WD[:, ((idg % 2) * 8 + ii) * 512 + hm * 128:
                                       ((idg % 2) * 8 + ii) * 512 + (hm + 1) * 128],
                                    HA[:, i * 512:(i + 1) * 512],
                                    start=False, stop=(ib == 7 and ii == 7))
                        ins.then_inc(dmi, 1)
                        idg += 1
                    t.drain().then_inc(dm, 1)
                    qg += 1
                t.wait_ge(dv, 16 * (tc + 1))
                for ts_ in range(4):
                    if tsg >= 1:
                        t.wait_ge(vE, 2 * tsg)
                    for u in range(16):
                        ins = t.transpose(
                            psE[:, (u // 8) * 1024 + (u % 8) * 128:
                                (u // 8) * 1024 + (u % 8 + 1) * 128],
                            DFT[:, u * 512 + ts_ * 128: u * 512 + ts_ * 128 + 128],
                            IDNd[:, :])
                        if u in (7, 15):
                            ins.then_inc(tm, 1)
                    tsg += 1

        @blk.scalar
        def _(s):
            s.wait_ge(mldd, 96)
            igl = 0
            qg = 0
            tsg = 0
            for tc in range(4):
                for sx in range(16):
                    for i in range(4):
                        gi = sx * 4 + i
                        pp = igl % 2
                        s.wait_ge(gm, 2 * igl + 1)
                        if igl >= 1:
                            s.wait_ge(hv, igl)
                        s.drain()
                        s.activation(GACT[:, :],
                                     psGU[:, (2 * pp) * 512:(2 * pp + 1) * 512],
                                     AF.Silu,
                                     scale=SGs[:, gi:gi + 1]).then_inc(sgd, 1)
                        igl += 1
                for q in range(4):
                    s.wait_ge(dm, qg + 1)
                    for hm in range(4):
                        hmg = q * 4 + hm
                        ins = s.activation(DFT[:, hmg * 512:(hmg + 1) * 512],
                                           psD[:, hm * 512:(hm + 1) * 512], AF.Copy,
                                           scale=SDs[:, hmg:hmg + 1])
                        ins.then_inc(dv, 1)
                    qg += 1
                for ts_ in range(4):
                    s.wait_ge(vE, 2 * tsg + 2)
                    if tsg >= 1:
                        s.wait_ge(sv4, 4 * (tsg - 1) + 4)
                    s.drain()
                    s.activation(LN4A[:, :],
                                 DTT[:, :],
                                 AF.Square, accum_out=st8d[:, 1:2]).then_inc(ss4, 1)
                    s.wait_ge(sv4, 4 * tsg + 2)
                    s.activation(st8d[:, 5:6], st8d[:, 4:5],
                                 AF.Sqrt).then_inc(ss4, 1)
                    s.wait_ge(sv4, 4 * tsg + 3)
                    s.activation(LN4A[:, :],
                                 DTT[:, :],
                                 AF.Identity, bias=st8d[:, 7:8],
                                 scale=st8d[:, 6:7]).then_inc(ss4, 1)
                    tsg += 1

        @blk.vector
        def _(v):
            v.wait_ge(mldd, 96)
            igl = 0
            tsg = 0
            for tc in range(4):
                v.wait_ge(csm, 2 * tc + 1)
                v.drain()
                v.tensor_scalar(NCSh[0:1, :], psD[0:1, 0:512], -128.0, None, ALU.mult)
                v.drain()
                v.scalar_tensor_tensor(NCSl[0:1, :], psD[0:1, 0:512], -128.0,
                                       NCSh[0:1, :], ALU.mult,
                                       ALU.subtract).then_inc(dcs, 1)
                for sx in range(16):
                    for i in range(4):
                        gi = sx * 4 + i
                        pp = igl % 2
                        v.wait_ge(gm, 2 * igl + 2)
                        v.drain()
                        v.tensor_scalar(UACT[:, :],
                                        psGU[:, (2 * pp + 1) * 512:(2 * pp + 2) * 512],
                                        SUs[:, gi:gi + 1],
                                        None, ALU.mult)
                        v.wait_ge(sgd, igl + 1)
                        v.drain()
                        v.tensor_tensor(HA[:, gi * 512:(gi + 1) * 512], GACT[:, :],
                                        UACT[:, :], ALU.mult).then_inc(hv, 1)
                        igl += 1
                v.wait_ge(csm, 2 * tc + 2)
                v.drain()
                v.tensor_scalar(NDh[0:1, :], psD[0:1, 0:512], -128.0, None, ALU.mult)
                v.drain()
                v.scalar_tensor_tensor(NDl[0:1, :], psD[0:1, 0:512], -128.0,
                                       NDh[0:1, :], ALU.mult,
                                       ALU.subtract).then_inc(dcs, 1)
                for ts_ in range(4):
                    for half in range(2):
                        v.wait_ge(tm, 2 * tsg + half + 1)
                        if tsg >= 1:
                            v.wait_ge(ss4, 3 * tsg)
                        v.tensor_copy(DTT[:, half * 1024:(half + 1) * 1024],
                                      psE[:, half * 1024:(half + 1) * 1024]
                                      ).then_inc(vE, 1)
                    if tsg >= 1:
                        v.wait_ge(ss4, 3 * tsg)
                    v.drain()
                    v.tensor_reduce(st8d[:, 0:1],
                                    DTT[:, :],
                                    AX.X, ALU.add, negate=True).then_inc(sv4, 1)
                    v.wait_ge(ss4, 3 * tsg + 1)
                    v.drain()
                    v.tensor_scalar(st8d[:, 2:3], st8d[:, 0:1], 1.0 / H, None, ALU.mult)
                    v.tensor_scalar(st8d[:, 4:5], st8d[:, 1:2], 1.0 / H, EPS,
                                    ALU.mult, op1=ALU.add)
                    v.drain()
                    v.tensor_tensor(st8d[:, 3:4], st8d[:, 2:3], st8d[:, 2:3], ALU.mult)
                    v.drain()
                    v.tensor_tensor(st8d[:, 4:5], st8d[:, 4:5], st8d[:, 3:4],
                                    ALU.subtract).then_inc(sv4, 1)
                    v.wait_ge(ss4, 3 * tsg + 2)
                    v.drain()
                    v.reciprocal(st8d[:, 6:7], st8d[:, 5:6])
                    v.drain()
                    v.tensor_tensor(st8d[:, 7:8], st8d[:, 2:3], st8d[:, 6:7],
                                    ALU.mult).then_inc(sv4, 1)
                    v.wait_ge(ss4, 3 * tsg + 3)
                    v.wait_ge(xl2[tsg % 2], 16 * (tsg // 2 + 1))
                    if tsg >= 1:
                        v.wait_ge(fo, 16 * tsg)
                    v.drain()
                    v.tensor_tensor(OUTT[:, :],
                                    X2R[:, :],
                                    LN4A[:, :], ALU.add).then_inc(sv4, 1)
                    tsg += 1

    return nc


_NC_CACHE = {}


def _get_nc():
    if "nc" not in _NC_CACHE:
        _NC_CACHE["nc"] = build_nc()
    return _NC_CACHE["nc"]


def _host_prep(inputs):
    x = np.asarray(inputs["L0_attn_x"], dtype=np.float32)
    qkv_w = np.ascontiguousarray(np.asarray(inputs["L0_attn_qkv_w"], np.float32))
    qkv_b = np.asarray(inputs["L0_attn_qkv_b"], np.float32)
    out_w = np.ascontiguousarray(np.asarray(inputs["L0_attn_out_w"], np.float32))
    out_b = np.asarray(inputs["L0_attn_out_b"], np.float32)
    gate_w = np.asarray(inputs["L0_ffn_gate_w"], np.int32)
    up_w = np.asarray(inputs["L0_ffn_up_w"], np.int32)
    down_w = np.asarray(inputs["L0_ffn_down_w"], np.int32)
    gate_sc = np.asarray(inputs["L0_ffn_gate_scale"], np.float32)
    up_sc = np.asarray(inputs["L0_ffn_up_scale"], np.float32)
    down_sc = np.asarray(inputs["L0_ffn_down_scale"], np.float32)

    def u8view(a_i32):
        a = np.ascontiguousarray(a_i32)
        return np.ascontiguousarray(
            a.view(np.uint8).reshape(a.shape[0], a.shape[1], 4)[:, :, 0])

    gate_u8 = u8view(gate_w.T)
    up_u8 = u8view(up_w.T)
    down_u8 = u8view(down_w.T)

    qkv_bp = np.ascontiguousarray(qkv_b.reshape(48, 128).T)
    out_b2 = np.zeros((2, H), np.float32)
    out_b2[0] = out_b
    sg_p = np.ascontiguousarray(gate_sc.reshape(64, 128).T)
    su_p = np.ascontiguousarray(up_sc.reshape(64, 128).T)
    sd_p = np.ascontiguousarray(down_sc.reshape(16, 128).T)

    slopes = np.array([1.0 / 2.0 ** (i / 2.0) for i in range(NH)], np.float32)
    alibi = np.zeros((NH, 2, S), dtype=ml_dtypes.bfloat16)
    j = np.arange(S, dtype=np.float32)
    for h in range(NH):
        a = slopes[h] * j
        hi = a.astype(ml_dtypes.bfloat16)
        lo = (a - hi.astype(np.float32)).astype(ml_dtypes.bfloat16)
        alibi[h, 0] = hi
        alibi[h, 1] = lo
    ident = np.eye(128, dtype=np.float32).astype(ml_dtypes.bfloat16)
    ones2 = np.ones((2, 128), dtype=ml_dtypes.bfloat16)
    onesc = np.ones((128, 1), dtype=ml_dtypes.bfloat16)

    masks = {}
    for par, groups in ((0, GROUPS_A), (1, GROUPS_B)):
        m = np.zeros((128, 16 * 1024), np.float32)
        for s_ in range(4):
            reg0 = SLOT_KEYS[s_] - 1024
            for r_ in range(4):
                tile = s_ * 4 + r_
                g = groups[s_][r_]
                d = g * 128 - reg0
                w = m[:, tile * 1024:(tile + 1) * 1024]
                for jc in range(1024):
                    e = jc - d
                    if 1 <= e <= 127:
                        w[e, jc] = NEG
                    elif e >= 128:
                        w[127, jc] = NEG
                        w[0, jc] = NEG
        masks[par] = np.ascontiguousarray(m.astype(ml_dtypes.bfloat16))
    msk_u = np.zeros((128, 128), np.float32)
    for k in range(128):
        msk_u[k, :k] = 1.0
    msk_u[0, 127] = 1.0
    msk_u = np.ascontiguousarray(msk_u.astype(ml_dtypes.bfloat16))

    in_maps = []
    own_rows_all = []
    for c in range(NCORE):
        b = c // 2
        par = c % 2
        groups = GROUPS_A if par == 0 else GROUPS_B
        own_tiles = [g for grp in groups for g in grp]
        own_rows = np.concatenate([np.arange(g * 128, (g + 1) * 128)
                                   for g in own_tiles])
        own_rows_all.append((b, own_rows))
        xb = np.ascontiguousarray(x[b])
        in_maps.append(dict(
            x_seq=xb,
            x_own=np.ascontiguousarray(xb[own_rows]),
            qkv_w=qkv_w, qkv_bp=qkv_bp, out_w=out_w, out_b2=out_b2,
            gate_u8=gate_u8, up_u8=up_u8, down_u8=down_u8,
            sg_p=sg_p, su_p=su_p, sd_p=sd_p,
            masks_in=masks[par], msku_in=msk_u, alibi_in=alibi,
            ident_in=ident, ones2_in=ones2, onesc_in=onesc,
        ))
    return in_maps, own_rows_all


def kernel(**inputs):
    in_maps, own_rows_all = _host_prep(inputs)
    nc = _get_nc()
    res = run_bass_kernel_spmd(nc, in_maps, core_ids=list(range(NCORE)))
    out = np.zeros((B, S, H), dtype=np.float32)
    for c in range(NCORE):
        b, own_rows = own_rows_all[c]
        out[b, own_rows, :] = np.asarray(res.results[c]["out"], np.float32)
    return out
